# version 05
# baseline (speedup 1.0000x reference)
import math
from contextlib import ExitStack

import numpy as np

import concourse.bass as bass
import concourse.mybir as mybir
from concourse.bass_utils import run_bass_kernel_spmd

F32 = mybir.dt.float32
BF16 = mybir.dt.bfloat16
AF = mybir.ActivationFunctionType
ALU = mybir.AluOpType

D = 4096
SEQ = 8192
NCORE = 8
TPC = 1024
NT = 8
SCALE = 1.0 / math.sqrt(128.0)
MASKV = -66504.0
EPS = 1e-5
INV_D = 1.0 / 4096.0
L_SHIFT = float(2.0 ** 24)

ENGS = ("tensor", "vector", "scalar", "gpsimd", "sync")
OWN = {"tensor": "pe", "vector": "ve", "scalar": "ac", "gpsimd": "po"}


class Sched:

    def __init__(self):
        self.streams = {e: [] for e in ENGS}
        self.n = {s: 0 for s in ("pe", "ve", "ac", "po", "qd", "gd",
                                 "ck", "cv", "cn")}

    def emit(self, eng, fn, inc=None, inc_by=1, waits=()):
        ws = [(s, v) for (s, v) in waits if v and v > 0 and s != OWN.get(eng)]
        self.streams[eng].append((fn, inc, inc_by, ws))
        if inc:
            self.n[inc] += inc_by
            return self.n[inc]
        return None

    def dma(self, eng, fn, waits=()):
        q = "qd" if eng == "sync" else "gd"
        return self.emit(eng, fn, inc=q, inc_by=16, waits=waits)

    def snap(self):
        return dict(self.n)

    def materialize(self, eng_obj, eng_name, sems):
        seen = {}
        for fn, inc, inc_by, ws in self.streams[eng_name]:
            for s, v in ws:
                if seen.get(s, 0) < v:
                    eng_obj.wait_ge(sems[s], v)
                    seen[s] = v
            ins = fn(eng_obj) if fn is not None else None
            if inc:
                if ins is None:
                    ins = eng_obj.nop()
                ins.then_inc(sems[inc], inc_by)


def build_program(debug=False):
    nc = bass.Bass("TRN2", target_bir_lowering=False, debug=False)

    def din(name, shape):
        return nc.dram_tensor(name, shape, F32, kind="ExternalInput").ap()

    x_fm = din("x_fm", [D, TPC])
    qkv_wt = din("qkv_wt", [96, 128, 4096])
    qkvb_fm = din("qkvb_fm", [128, 96])
    out_wt = din("out_wt", [32, 128, 4096])
    outb_fm = din("outb_fm", [128, 32])
    ffn_wtt = din("ffn_wtt", [32, 128, 4096])
    ffnb_fm = din("ffnb_fm", [128, 32])
    a_allt_t = din("a_allt_t", [2, 128, 4096])
    bfr_t = din("bfr_t", [2, 128, 4096])
    cf_in = din("cf_in", [16, 1])
    sc_in = din("sc_in", [16, 1])
    cs_in = din("cs_in", [128, TPC])
    csw_in = din("csw_in", [128, TPC])
    swap_in = din("swap_in", [128, 128])
    maskT_in = din("maskT_in", [128, 1024])
    id_in = din("id_in", [128, 128])
    oh_in = din("oh_in", [16, 256])

    out_fm = nc.dram_tensor("out_fm", [D, TPC], F32, kind="ExternalOutput").ap()

    def dint(name, shape, dt=BF16):
        return nc.dram_tensor(name, shape, dt, kind="Internal").ap()

    q_spill = dint("q_spill", [D, TPC])
    k_in = dint("k_in", [D, TPC])
    v_in = dint("v_in", [TPC, D])
    k_all = dint("k_all", [NCORE * D, TPC])
    v_all = dint("v_all", [SEQ, D])
    kn_i = dint("kn_i", [1, 32], F32)
    knm_i = dint("knm_i", [1, 32], F32)
    x2_sp = dint("x2_sp", [D, TPC], F32)

    dbg = {}
    if debug:
        for nm, shape, dt in [
            ("d_ln1", [128, 32 * 1024], BF16), ("d_ctx", [128, 32 * 1024], BF16),
            ("d_kall", [NCORE * D, TPC], BF16), ("d_vall", [SEQ, D], BF16),
            ("d_qsp", [D, TPC], BF16), ("d_bias", [128, 32], F32),
            ("d_x2", [D, TPC], F32), ("d_ln2", [128, 32 * 1024], BF16),
        ]:
            dbg[nm] = nc.dram_tensor(nm, shape, dt, kind="ExternalOutput").ap()

    with ExitStack() as stk:
        E = stk.enter_context
        AB = E(nc.sbuf_tensor("AB", [128, 49152], BF16))
        CTX = E(nc.sbuf_tensor("CTX", [128, 32768], BF16))
        AFR = E(nc.sbuf_tensor("AFR", [128, 3072], F32))
        cs_sb = E(nc.sbuf_tensor("cs_sb", [128, TPC], F32))
        csw_sb = E(nc.sbuf_tensor("csw_sb", [128, TPC], F32))
        maskT = E(nc.sbuf_tensor("maskT", [128, 1024], F32))
        id_sb = E(nc.sbuf_tensor("id_sb", [128, 128], BF16))
        swp_sb = E(nc.sbuf_tensor("swp_sb", [128, 128], BF16))
        qkvb = E(nc.sbuf_tensor("qkvb", [128, 96], F32))
        outb = E(nc.sbuf_tensor("outb", [128, 32], F32))
        ffnb = E(nc.sbuf_tensor("ffnb", [128, 32], F32))
        bias_sb = E(nc.sbuf_tensor("bias_sb", [128, 32], F32))
        wer_sb = E(nc.sbuf_tensor("wer_sb", [128, 2], F32))
        oh_sb = E(nc.sbuf_tensor("oh_sb", [16, 256], F32))
        cf_sb = E(nc.sbuf_tensor("cf_sb", [16, 1], F32))
        sc_sb = E(nc.sbuf_tensor("sc_sb", [16, 1], F32))
        w16_sb = E(nc.sbuf_tensor("w16_sb", [16, 1], F32))
        ones_cf = E(nc.sbuf_tensor("ones_cf", [128, 1], F32))
        ones_cb = E(nc.sbuf_tensor("ones_cb", [128, 1], BF16))
        ones_rf = E(nc.sbuf_tensor("ones_rf", [1, 128], F32))
        ROWS = E(nc.sbuf_tensor("ROWS", [1, 4096], F32))
        ROW2 = E(nc.sbuf_tensor("ROW2", [1, 256], F32))
        CST = E(nc.sbuf_tensor("CST", [1, 8], F32))

        P = [E(nc.psum_tensor(f"P{i}", [128, 1024], F32)) for i in range(4)]
        P2bf = P[2].ap().bitcast(BF16)

        sems = {}
        for s in ("pe", "ve", "ac", "po", "qd", "gd", "ck", "cv", "cn"):
            sems[s] = E(nc.semaphore(s + "_s"))
        block = E(nc.Block())

        S = Sched()

        r_mean = ROWS[0:1, 0:1024]
        r_tmp = ROWS[0:1, 1024:2048]
        r_rstd = ROWS[0:1, 2048:3072]
        r_brow = ROWS[0:1, 3072:4096]
        r_kn = ROW2[0:1, 0:32]
        r_knm = ROW2[0:1, 32:64]
        r_qn = ROW2[0:1, 64:96]
        r_b1 = ROW2[0:1, 96:128]
        r_b2 = ROW2[0:1, 128:160]
        r_b3 = ROW2[0:1, 160:192]

        def ab(c0, w):
            return AB[:, c0:c0 + w]

        def afr(c0, w):
            return AFR[:, c0:c0 + w]

        LN1 = lambda kc: ab(kc * 1024, 1024)
        WST = [ab(32768, 4096), ab(36864, 4096)]
        ROPE = [ab(40960, 1024), ab(41984, 1024)]
        VSTG = [ab(43008, 1024), ab(44032, 1024)]
        VTM = [ab(45056, 1024), ab(46080, 1024)]
        SQK = ab(47104, 1024)
        KB = [ab(0, 8192), ab(8192, 8192)]
        VB = [ab(16384, 8192), ab(24576, 8192), ab(32768, 8192)]
        QB = [ab(40960, 1024), ab(41984, 1024)]
        PT = [ab(43008 + i * 1024, 1024) for i in range(4)]
        XA = [ab(0, 1024), ab(1024, 1024)]
        BFR = [ab(2048, 4096), ab(6144, 4096)]
        AAT = ab(10240, 8192)

        XLD = [afr(0, 1024), afr(1024, 1024)]
        TA = XLD[0]
        TB = XLD[1]
        SQF = afr(2048, 1024)
        RLB = afr(2048, 1024)
        X2S = XLD

        for dst, src in ((cs_sb, cs_in), (csw_sb, csw_in), (qkvb, qkvb_fm),
                         (outb, outb_fm), (ffnb, ffnb_fm), (oh_sb, oh_in),
                         (cf_sb, cf_in), (sc_sb, sc_in), (maskT, maskT_in)):
            S.dma("sync", lambda sy, d=dst, s=src: sy.dma_start(out=d[:, :], in_=s))
        pre_qd = S.n["qd"]
        for dst, src in ((id_sb, id_in), (swp_sb, swap_in)):
            S.dma("gpsimd", lambda g, d=dst, s=src: g.dma_start(out=d[:, :], in_=s))
        S.emit("gpsimd", lambda g: g.memset(CST[0:1, 0:1], EPS))
        S.emit("gpsimd", lambda g: g.memset(CST[0:1, 1:2], SCALE))
        nc.const_aps.register(F32, EPS, CST[0:1, 0:1])
        nc.const_aps.register(F32, SCALE, CST[0:1, 1:2])
        S.emit("gpsimd", lambda g: g.memset(ones_cf[:, :], 1.0))
        S.emit("gpsimd", lambda g: g.memset(ones_cb[:, :], 1.0))
        pre_po = S.emit("gpsimd", lambda g: g.memset(ones_rf[:, :], 1.0), inc="po")

        xq = {}
        stat_pe = {}
        sq_ac = {}
        for c in range(32):
            sl = c % 2
            w = [("pe", stat_pe[c - 2]), ("ac", sq_ac[c - 2])] if c >= 2 else []
            xq[c] = S.dma("sync", lambda sy, c=c, sl=sl: sy.dma_start(
                out=XLD[sl][:, :], in_=x_fm[c * 128:(c + 1) * 128, :]), waits=w)
            wsq = [("qd", xq[c])]
            if c >= 1:
                wsq.append(("pe", stat_pe[c - 1]))
            if c == 0:
                wsq.append(("po", pre_po))
            sq_ac[c] = S.emit("scalar", lambda s, sl=sl: s.activation(
                SQF[:, :], XLD[sl][:, :], AF.Square), inc="ac", waits=wsq)
            st0, sp0 = (c == 0), (c == 31)
            S.emit("tensor", lambda t, sl=sl, st0=st0, sp0=sp0: t.matmul(
                P[2][0:1, 0:512], ones_cf[:, :], XLD[sl][:, 0:512], start=st0, stop=sp0),
                waits=[("qd", xq[c]), ("po", pre_po)])
            S.emit("tensor", lambda t, sl=sl, st0=st0, sp0=sp0: t.matmul(
                P[2][0:1, 512:1024], ones_cf[:, :], XLD[sl][:, 512:1024], start=st0, stop=sp0))
            S.emit("tensor", lambda t, st0=st0, sp0=sp0: t.matmul(
                P[3][0:1, 0:512], ones_cf[:, :], SQF[:, 0:512], start=st0, stop=sp0),
                waits=[("ac", sq_ac[c])])
            stat_pe[c] = S.emit("tensor", lambda t, st0=st0, sp0=sp0: t.matmul(
                P[3][0:1, 512:1024], ones_cf[:, :], SQF[:, 512:1024], start=st0, stop=sp0),
                inc="pe")

        def emit_ln_rows(stat_wait):
            S.emit("vector", lambda v: v.tensor_scalar_mul(
                r_mean[:, :], P[2][0:1, 0:1024], INV_D), waits=[("pe", stat_wait)])
            S.emit("vector", lambda v: v.tensor_mul(r_tmp[:, :], r_mean[:, :], r_mean[:, :]))
            vv = S.emit("vector", lambda v: v.scalar_tensor_tensor(
                r_brow[:, :], P[3][0:1, 0:1024], INV_D, r_tmp[:, :], ALU.mult, ALU.subtract),
                inc="ve")
            aa = S.emit("scalar", lambda s: s.activation(
                r_tmp[:, :], r_brow[:, :], AF.Sqrt, bias=EPS), inc="ac", waits=[("ve", vv)])
            S.emit("vector", lambda v: v.reciprocal(r_rstd[:, :], r_tmp[:, :]),
                   waits=[("ac", aa)])
            rw = S.emit("vector", lambda v: v.tensor_mul(
                r_brow[:, :], r_mean[:, :], r_rstd[:, :]), inc="ve")
            S.emit("tensor", lambda t: t.matmul(
                P[2][:, 0:512], ones_rf[:, :], r_rstd[:, 0:512], start=True, stop=True),
                waits=[("ve", rw), ("po", pre_po)])
            S.emit("tensor", lambda t: t.matmul(
                P[2][:, 512:1024], ones_rf[:, :], r_rstd[:, 512:1024], start=True, stop=True))
            S.emit("tensor", lambda t: t.matmul(
                P[3][:, 0:512], ones_rf[:, :], r_brow[:, 0:512], start=True, stop=True))
            return S.emit("tensor", lambda t: t.matmul(
                P[3][:, 512:1024], ones_rf[:, :], r_brow[:, 512:1024], start=True, stop=True),
                inc="pe")

        bcast_pe = emit_ln_rows(stat_pe[31])

        ln1_ve = {}
        for c in range(32):
            sl = c % 2
            w = [("pe", bcast_pe)]
            if c >= 2:
                w.append(("ve", ln1_ve[c - 2]))
            xq2 = S.dma("sync", lambda sy, c=c, sl=sl: sy.dma_start(
                out=XLD[sl][:, :], in_=x_fm[c * 128:(c + 1) * 128, :]), waits=w)
            S.emit("vector", lambda v, sl=sl: v.tensor_mul(
                SQF[:, :], XLD[sl][:, :], P[2][:, :]),
                waits=[("qd", xq2), ("pe", bcast_pe)])
            ln1_ve[c] = S.emit("vector", lambda v, c=c: v.tensor_sub(
                LN1(c)[:, :], SQF[:, :], P[3][:, :]), inc="ve")
        ln1_done = ln1_ve[31]
        if debug:
            S.dma("sync", lambda sy: sy.dma_start(out=dbg["d_ln1"], in_=AB[:, 0:32768]),
                  waits=[("ve", ln1_done)])

        qkv_mm = {}
        chunk_free = {}
        last_rope_dma = {}
        last_sqk = {"mm": 0, "rd": ln1_done}
        last_rope_mm = {"pe": 0, "ve": 0, "evac": 0}
        wld = {}

        CTXF = CTX.ap().bitcast(F32)
        WSTG = [CTXF[:, i * 4096:(i + 1) * 4096] for i in range(4)]
        wcast = {}

        def emit_wload(idx, c):
            wfree = [("ac", wcast[idx - 4])] if idx >= 4 else []
            wld[idx] = S.dma("sync", lambda sy, c=c, sl=idx % 4: sy.dma_start(
                out=WSTG[sl][:, :], in_=qkv_wt[c, :, :]), waits=wfree)

        def emit_wcast(idx):
            wc = [("qd", wld[idx])]
            if idx >= 2:
                wc.append(("pe", qkv_mm[idx - 2]))
            wcast[idx] = S.emit("scalar", lambda sc, sl4=idx % 4, sl=idx % 2:
                                sc.activation(WST[sl][:, :], WSTG[sl4][:, :],
                                              AF.Identity), inc="ac", waits=wc)

        def emit_wmms(idx):
            psl = idx % 2
            w0 = [("ac", wcast[idx]), ("ve", ln1_done)]
            if idx >= 2:
                w0.append(chunk_free[idx - 2])
            last = None
            for kc in range(32):
                st0, sp0 = (kc == 0), (kc == 31)
                S.emit("tensor", lambda t, sl=psl, kc=kc, st0=st0, sp0=sp0:
                       t.matmul(P[sl][:, 0:512], WST[sl][:, kc * 128:(kc + 1) * 128],
                                LN1(kc)[:, 0:512], start=st0, stop=sp0),
                       waits=(w0 if kc == 0 else []))
                last = S.emit("tensor", lambda t, sl=psl, kc=kc, st0=st0, sp0=sp0:
                              t.matmul(P[sl][:, 512:1024], WST[sl][:, kc * 128:(kc + 1) * 128],
                                       LN1(kc)[:, 512:1024], start=st0, stop=sp0),
                              inc="pe" if sp0 else None)
            qkv_mm[idx] = last

        def emit_rope_a(idx, c):
            psl = idx % 2
            w = [("pe", qkv_mm[idx]), ("pe", last_rope_mm["pe"])]
            S.emit("vector", lambda v, psl=psl, c=c: v.scalar_tensor_tensor(
                VSTG[0][:, :], P[psl][:, :], qkvb[:, c:c + 1], cs_sb[:, :],
                ALU.add, ALU.mult), waits=w)
            rv = S.emit("vector", lambda v, psl=psl, c=c: v.scalar_tensor_tensor(
                VSTG[1][:, :], P[psl][:, :], qkvb[:, c:c + 1], csw_sb[:, :],
                ALU.add, ALU.mult), inc="ve")
            chunk_free[idx] = ("ve", rv)
            last_rope_mm["ve"] = rv

        def emit_rope_b(idx, h, dest_dram, nrow):
            sl = idx % 2
            wmm = [("ve", last_rope_mm["ve"]), ("ac", last_rope_mm["evac"])]
            S.emit("tensor", lambda t: t.matmul(
                P[3][:, 0:512], id_sb[:, :], VSTG[0][:, 0:512], start=True, stop=False),
                waits=wmm)
            S.emit("tensor", lambda t: t.matmul(
                P[3][:, 0:512], swp_sb[:, :], VSTG[1][:, 0:512], start=False, stop=True))
            S.emit("tensor", lambda t: t.matmul(
                P[3][:, 512:1024], id_sb[:, :], VSTG[0][:, 512:1024], start=True, stop=False))
            rmm = S.emit("tensor", lambda t: t.matmul(
                P[3][:, 512:1024], swp_sb[:, :], VSTG[1][:, 512:1024], start=False, stop=True),
                inc="pe")
            last_rope_mm["pe"] = rmm
            wev = [("pe", rmm)]
            if sl in last_rope_dma:
                wev.append(("qd", last_rope_dma[sl]))
            ev = S.emit("scalar", lambda s, sl=sl: s.activation(
                ROPE[sl][:, :], P[3][:, :], AF.Identity), inc="ac", waits=wev)
            last_rope_mm["evac"] = ev
            last_rope_dma[sl] = S.dma("sync", lambda sy, sl=sl, h=h, dd=dest_dram:
                                      sy.dma_start(out=dd[h * 128:(h + 1) * 128, :],
                                                   in_=ROPE[sl][:, :]),
                                      waits=[("ac", ev)])
            wsq = [("ac", ev), ("pe", last_sqk["mm"])]
            sq = S.emit("scalar", lambda s, sl=sl: s.activation(
                SQK[:, :], ROPE[sl][:, :], AF.Square), inc="ac", waits=wsq)
            last_rope_dma[(sl, "sq")] = sq
            wmm2 = [("ac", sq), ("ve", last_sqk["rd"])]
            S.emit("tensor", lambda t: t.matmul(
                P[2][0:1, 0:512], ones_cb[:, :], SQK[:, 0:512], start=True, stop=True),
                waits=wmm2)
            last_sqk["mm"] = S.emit("tensor", lambda t: t.matmul(
                P[2][0:1, 512:1024], ones_cb[:, :], SQK[:, 512:1024], start=True, stop=True),
                inc="pe")
            last_sqk["rd"] = S.emit("vector", lambda v, h=h, nrow=nrow: v.reduce_max(
                nrow[:, h:h + 1], P[2][0:1, 0:1024], axis=mybir.AxisListType.X),
                inc="ve", waits=[("pe", last_sqk["mm"])])

        qkvcol = lambda idx: (32 + idx) if idx < 32 else ((64 + idx - 32) if idx < 64 else idx - 64)
        emit_wload(0, qkvcol(0))
        emit_wload(1, qkvcol(1))
        emit_wcast(0)
        for i in range(32):
            if i + 2 < 96:
                emit_wload(i + 2, qkvcol(i + 2))
            emit_wmms(i)
            if i >= 1:
                emit_rope_b(i - 1, i - 1, k_in, r_kn)
            emit_rope_a(i, 32 + i)
            emit_wcast(i + 1)
        emit_rope_b(31, 31, k_in, r_kn)
        k_done_qd = last_rope_dma[1]
        kn_qd = S.dma("sync", lambda sy: sy.dma_start(out=kn_i, in_=r_kn),
                      waits=[("ve", last_sqk["rd"])])
        k_nrm_rd = last_sqk["rd"]

        vtm_dma = {}
        vtm_ac = {}

        def emit_v_evac(i):
            idx = 32 + i
            h = i
            psl = idx % 2
            ve_ac = S.emit("scalar", lambda s, psl=psl, c=64 + i: s.activation(
                VSTG[psl][:, :], P[psl][:, :], AF.Identity, bias=qkvb[:, c:c + 1]),
                inc="ac", waits=[("pe", qkv_mm[idx])])
            chunk_free[idx] = ("ac", ve_ac)
            wtr = [("ac", ve_ac)]
            if i == 0:
                wtr.append(("ve", k_nrm_rd))
            if i >= 1:
                wtr.append(("ac", vtm_ac[i - 1]))
            last = None
            for j in range(8):
                last = S.emit("tensor", lambda t, psl=psl, j=j: t.transpose(
                    P2bf[:, j * 128:(j + 1) * 128], VSTG[psl][:, j * 128:(j + 1) * 128],
                    id_sb[:, :]), inc="pe" if j == 7 else None,
                    waits=(wtr if j == 0 else []))
            wvt = [("pe", last)]
            if i >= 2:
                wvt.append(("qd", vtm_dma[i - 2]))
            vtm_ac[i] = S.emit("scalar", lambda s, psl=psl: s.activation(
                VTM[psl][:, :], P2bf[:, 0:1024], AF.Identity), inc="ac", waits=wvt)
            vtm_dma[i] = S.dma("sync", lambda sy, psl=psl, h=h: sy.dma_start(
                out=v_in.rearrange("(j p) o -> p j o", p=128)[:, :, h * 128:(h + 1) * 128],
                in_=VTM[psl][:, :].rearrange("p (j o) -> p j o", j=8)),
                waits=[("ac", vtm_ac[i])])

        for i in range(32):
            idx = 32 + i
            if idx + 2 < 96:
                emit_wload(idx + 2, qkvcol(idx + 2))
            emit_wmms(idx)
            if i >= 1:
                emit_v_evac(i - 1)
            emit_wcast(idx + 1)
        emit_v_evac(31)
        v_done_qd = vtm_dma[31]

        for i in range(32):
            idx = 64 + i
            if idx + 2 < 96:
                emit_wload(idx + 2, qkvcol(idx + 2))
            emit_wmms(idx)
            if i >= 1:
                emit_rope_b(idx - 1, i - 1, q_spill, r_qn)
            emit_rope_a(idx, i)
            if idx + 1 < 96:
                emit_wcast(idx + 1)
        emit_rope_b(95, 31, q_spill, r_qn)
        q_nrm_rd = last_sqk["rd"]

        S.emit("gpsimd", lambda g: g.collective_compute(
            "AllGather", ALU.bypass, replica_groups=[list(range(NCORE))],
            ins=[k_in.tensor.ap().opt()], outs=[k_all.tensor.ap().opt()]),
            inc="ck", waits=[("qd", k_done_qd), ("pe", S.n["pe"]),
                             ("ve", S.n["ve"]), ("ac", S.n["ac"]),
                             ("qd", S.n["qd"])])
        S.emit("gpsimd", lambda g: g.collective_compute(
            "AllGather", ALU.bypass, replica_groups=[list(range(NCORE))],
            ins=[v_in.tensor.ap().opt()], outs=[v_all.tensor.ap().opt()]),
            inc="cv")
        S.emit("gpsimd", lambda g: g.collective_compute(
            "AllReduce", ALU.max, replica_groups=[list(range(NCORE))],
            ins=[kn_i.tensor.ap().opt()], outs=[knm_i.tensor.ap().opt()]),
            inc="cn")





        knm_qd = S.dma("sync", lambda sy: sy.dma_start(
            out=r_knm, in_=knm_i), waits=[("cn", 1)])
        bb = S.emit("vector", lambda v: v.tensor_mul(
            r_b1[:, :], r_qn[:, :], r_knm[:, :]),
            inc="ve", waits=[("qd", knm_qd), ("ve", q_nrm_rd)])
        bs = S.emit("scalar", lambda s: s.activation(
            r_b2[:, :], r_b1[:, :], AF.Sqrt), inc="ac", waits=[("ve", bb)])
        bv = S.emit("vector", lambda v: v.tensor_scalar_mul(
            r_b3[:, :], r_b2[:, :], -SCALE), inc="ve", waits=[("ac", bs)])
        bm = S.emit("tensor", lambda t: t.matmul(
            P[2][:, 0:32], ones_rf[:, :], r_b3[:, :], start=True, stop=True),
            inc="pe", waits=[("ve", bv), ("ve", q_nrm_rd)])
        bias_ac = S.emit("scalar", lambda s: s.activation(
            bias_sb[:, :], P[2][:, 0:32], AF.Identity), inc="ac", waits=[("pe", bm)])
        if debug:
            S.dma("sync", lambda sy: sy.dma_start(out=dbg["d_bias"], in_=bias_sb[:, :]),
                  waits=[("ac", bias_ac)])
            S.dma("sync", lambda sy: sy.dma_start(out=dbg["d_qsp"], in_=q_spill),
                  waits=[("qd", last_rope_dma[1])])
            S.dma("sync", lambda sy: sy.dma_start(out=dbg["d_kall"], in_=k_all),
                  waits=[("ck", 1)])
            S.dma("sync", lambda sy: sy.dma_start(out=dbg["d_vall"], in_=v_all),
                  waits=[("cv", 1)])

        a_end = S.snap()

        kq_qd = {}
        v_qd = {}
        half_last_pv = {}
        exp_ac = {}
        mask_ve = {}
        pv_pe = {}
        ctx_ve = {}
        rlb_ac = {}
        head_last_pe = {}

        for p16 in range(16):
            for hh in range(2):
                h = 2 * p16 + hh
                wk = [("ck", 1), ("cv", 1), ("pe", a_end["pe"]),
                      ("ve", a_end["ve"]), ("ac", a_end["ac"])] if h < 2 else \
                     [("pe", head_last_pe[h - 2])]
                for b in range(8):
                    S.dma("sync", lambda sy, h=h, b=b: sy.dma_start(
                        out=KB[h % 2][:, b * 1024:(b + 1) * 1024],
                        in_=k_all[b * D + h * 128: b * D + (h + 1) * 128, :]),
                        waits=(wk if b == 0 else []))
                kq_qd[h] = S.dma("sync", lambda sy, h=h: sy.dma_start(
                    out=QB[h % 2][:, :], in_=q_spill[h * 128:(h + 1) * 128, :]))
            for a in range(8):
                g = 2 * p16 + (a // 4)
                vw = [("cv", 1)]
                if g < 2:
                    vw += [("pe", a_end["pe"]), ("ve", a_end["ve"]), ("ac", a_end["ac"])]
                else:
                    vw.append(("pe", half_last_pv.get(g - 3, 0)))
                v_qd[(p16, a)] = S.dma("sync", lambda sy, p16=p16, a=a, sl=(2 * p16 + a // 4) % 3:
                                       sy.dma_start(
                    out=VB[sl][:, (a % 4) * 2048:(a % 4 + 1) * 2048]
                        .rearrange("p (b c) -> p b c", b=8),
                    in_=v_all.rearrange("(b a p) o -> b a p o", a=8, p=128)
                        [:, a, :, p16 * 256:(p16 + 1) * 256].rearrange("b p o -> p b o")),
                    waits=vw)

            for hh in range(2):
                h = 2 * p16 + hh
                qb = QB[h % 2]
                kb = KB[h % 2]

                def emit_qk(h, hh, p16, a, b, qb=qb, kb=kb):
                    gidx = h * 64 + a * 8 + b
                    par = gidx % 2
                    suf = (8 - a) * 128
                    rstar = 512 - a * 128
                    s0 = P[0][:, par * 512:(par + 1) * 512]
                    s1 = P[1][:, par * 512:(par + 1) * 512]
                    kt = b * 1024 + a * 128
                    slot = gidx % 4
                    wqk = [("qd", kq_qd[h])]
                    if gidx >= 2:
                        wqk.append(("ac", exp_ac[gidx - 2]))
                        wqk.append(("ve", mask_ve[gidx - 2]))
                    else:
                        wqk += [("ve", a_end["ve"]), ("ac", a_end["ac"])]
                    if h > 0 and a == 0 and b == 0:
                        wqk += [("ve", ctx_ve[h - 1]), ("ac", rlb_ac[h - 1])]
                    if a <= 3:
                        S.emit("tensor", lambda t, s0=s0, kt=kt, a=a, rs=rstar:
                               t.matmul(s0[:, 0:rs], kb[:, kt:kt + 128],
                                        qb[:, a * 128:512], start=True, stop=True),
                               waits=wqk)
                        qk_pe = S.emit("tensor", lambda t, s1=s1, kt=kt:
                                       t.matmul(s1[:, 0:512], kb[:, kt:kt + 128],
                                                qb[:, 512:1024], start=True, stop=True),
                                       inc="pe")
                    else:
                        qk_pe = S.emit("tensor", lambda t, s0=s0, kt=kt, a=a, suf=suf:
                                       t.matmul(s0[:, 0:suf], kb[:, kt:kt + 128],
                                                qb[:, a * 128:1024], start=True, stop=True),
                                       inc="pe", waits=wqk)
                    mask_ve[gidx] = S.emit("vector", lambda v, s0=s0, b=b: v.tensor_add(
                        s0[:, 0:128], s0[:, 0:128], maskT[:, b * 128:(b + 1) * 128]),
                        inc="ve", waits=[("pe", qk_pe)])
                    we = [("pe", qk_pe), ("ve", mask_ve[gidx])]
                    if gidx >= 4:
                        we.append(("pe", pv_pe.get(gidx - 4, 0)))
                    else:
                        we.append(("qd", v_done_qd))
                    if a <= 3:
                        S.emit("scalar", lambda sc, s0=s0, slot=slot, h=h, rs=rstar:
                               sc.activation(PT[slot][:, 0:rs], s0[:, 0:rs], AF.Exp,
                                             bias=bias_sb[:, h:h + 1], scale=SCALE),
                               waits=we)
                        exp_ac[gidx] = S.emit(
                            "scalar", lambda sc, s1=s1, slot=slot, h=h, rs=rstar, suf=suf:
                            sc.activation(PT[slot][:, rs:suf], s1[:, 0:512], AF.Exp,
                                          bias=bias_sb[:, h:h + 1], scale=SCALE),
                            inc="ac")
                    else:
                        exp_ac[gidx] = S.emit(
                            "scalar", lambda sc, s0=s0, slot=slot, h=h, suf=suf:
                            sc.activation(PT[slot][:, 0:suf], s0[:, 0:suf], AF.Exp,
                                          bias=bias_sb[:, h:h + 1], scale=SCALE),
                            inc="ac", waits=we)

                def emit_lpv(h, hh, p16, a, b):
                    gidx = h * 64 + a * 8 + b
                    suf = (8 - a) * 128
                    rstar = 512 - a * 128
                    slot = gidx % 4
                    vsl = (2 * p16 + a // 4) % 3
                    voff = (a % 4) * 2048 + b * 256 + hh * 128
                    wl = [("ac", exp_ac[gidx]), ("qd", v_qd[(p16, a)])]
                    first = (a == 0 and b == 0)
                    lastc = (a == 7 and b == 7)
                    if a <= 3:
                        S.emit("tensor", lambda t, slot=slot, a=a, rs=rstar, first=first, b=b:
                               t.matmul(P[3][0:1, a * 128:512], ones_cb[:, :],
                                        PT[slot][:, 0:rs], start=first,
                                        stop=(a == 3 and b == 7)), waits=wl)
                        S.emit("tensor", lambda t, slot=slot, rs=rstar, suf=suf, first=first, lastc=lastc:
                               t.matmul(P[3][0:1, 512:1024], ones_cb[:, :],
                                        PT[slot][:, rs:suf], start=first, stop=lastc))
                        S.emit("tensor", lambda t, slot=slot, a=a, rs=rstar, vsl=vsl, voff=voff, first=first, b=b:
                               t.matmul(P[2][:, a * 128:512], VB[vsl][:, voff:voff + 128],
                                        PT[slot][:, 0:rs], start=first,
                                        stop=(a == 3 and b == 7)))
                        pv_pe[gidx] = S.emit(
                            "tensor", lambda t, slot=slot, rs=rstar, suf=suf, vsl=vsl, voff=voff, first=first, lastc=lastc:
                            t.matmul(P[2][:, 512:1024], VB[vsl][:, voff:voff + 128],
                                     PT[slot][:, rs:suf], start=first, stop=lastc),
                            inc="pe")
                    else:
                        S.emit("tensor", lambda t, slot=slot, a=a, suf=suf, lastc=lastc:
                               t.matmul(P[3][0:1, a * 128:1024], ones_cb[:, :],
                                        PT[slot][:, 0:suf], start=False, stop=lastc),
                               waits=wl)
                        pv_pe[gidx] = S.emit(
                            "tensor", lambda t, slot=slot, a=a, suf=suf, vsl=vsl, voff=voff, lastc=lastc:
                            t.matmul(P[2][:, a * 128:1024], VB[vsl][:, voff:voff + 128],
                                     PT[slot][:, 0:suf], start=False, stop=lastc),
                            inc="pe")

                chunks = [(a, b) for a in range(8) for b in range(8)]
                for ci, (a, b) in enumerate(chunks):
                    emit_qk(h, hh, p16, a, b)
                    if ci >= 1:
                        pa, pb = chunks[ci - 1]
                        emit_lpv(h, hh, p16, pa, pb)
                emit_lpv(h, hh, p16, 7, 7)
                for a in range(8):
                    g = 2 * p16 + (a // 4)
                    half_last_pv[g] = pv_pe[h * 64 + a * 8 + 7]
                head_last_pe[h] = pv_pe[h * 64 + 63]
                hl = head_last_pe[h]
                S.emit("vector", lambda v: v.tensor_scalar_mul(
                    r_tmp[:, 0:512], P[3][0:1, 0:512], L_SHIFT), waits=[("pe", hl)])
                S.emit("vector", lambda v: v.tensor_scalar_mul(
                    r_tmp[:, 512:1024], P[3][0:1, 512:1024], L_SHIFT))
                S.emit("vector", lambda v: v.reciprocal(r_rstd[:, :], r_tmp[:, :]))
                rl = S.emit("vector", lambda v: v.tensor_scalar_mul(
                    r_brow[:, :], r_rstd[:, :], L_SHIFT), inc="ve")
                S.emit("tensor", lambda t: t.matmul(
                    P[3][:, 0:512], ones_rf[:, :], r_brow[0:1, 0:512], start=True, stop=True),
                    waits=[("ve", rl)])
                rb = S.emit("tensor", lambda t: t.matmul(
                    P[3][:, 512:1024], ones_rf[:, :], r_brow[0:1, 512:1024], start=True, stop=True),
                    inc="pe")
                wrlb = [("pe", rb)]
                if h >= 1:
                    wrlb.append(("ve", ctx_ve[h - 1]))
                else:
                    wrlb.append(("ve", a_end["ve"]))
                rlb_ac[h] = S.emit("scalar", lambda sc: sc.activation(
                    RLB[:, :], P[3][:, :], AF.Identity), inc="ac", waits=wrlb)
                S.emit("vector", lambda v, h=h: v.tensor_mul(
                    CTX[:, h * 1024: h * 1024 + 512], P[2][:, 0:512], RLB[:, 0:512]),
                    waits=[("ac", rlb_ac[h])])
                ctx_ve[h] = S.emit("vector", lambda v, h=h: v.tensor_mul(
                    CTX[:, h * 1024 + 512: (h + 1) * 1024], P[2][:, 512:1024], RLB[:, 512:1024]),
                    inc="ve")
            pair_last = head_last_pe[2 * p16 + 1]
        ctx_all = ctx_ve[31]
        b_end_pe = head_last_pe[31]
        if debug:
            S.dma("sync", lambda sy: sy.dma_start(out=dbg["d_ctx"], in_=CTX[:, :]),
                  waits=[("ve", ctx_all)])

        cmm = {}
        x2_ve = {}
        x2w = {}
        csq = {}
        cstat = {}
        for of in range(32):
            par = of % 2
            wfree = [("pe", cmm[of - 2])] if of >= 2 else \
                    [("pe", b_end_pe), ("ve", ctx_all), ("ac", rlb_ac[31])]
            cwld = S.dma("gpsimd", lambda g, of=of, par=par: g.dma_start(
                out=WST[par][:, :], in_=out_wt[of, :, :]), waits=wfree)
            wx = [("ve", x2_ve[of - 2])] if of >= 2 else [("ve", ctx_all)]
            cxld = S.dma("sync", lambda sy, of=of, par=par: sy.dma_start(
                out=XLD[par][:, :], in_=x_fm[of * 128:(of + 1) * 128, :]), waits=wx)
            w0 = [("gd", cwld), ("ve", ctx_all)]
            if of >= 2:
                w0.append(("ve", x2_ve[of - 2]))
            for kc in range(32):
                st0, sp0 = (kc == 0), (kc == 31)
                S.emit("tensor", lambda t, par=par, kc=kc, st0=st0, sp0=sp0: t.matmul(
                    P[par][:, 0:512], WST[par][:, kc * 128:(kc + 1) * 128],
                    CTX[:, kc * 1024: kc * 1024 + 512], start=st0, stop=sp0),
                    waits=(w0 if kc == 0 else []))
                cmm[of] = S.emit("tensor", lambda t, par=par, kc=kc, st0=st0, sp0=sp0: t.matmul(
                    P[par][:, 512:1024], WST[par][:, kc * 128:(kc + 1) * 128],
                    CTX[:, kc * 1024 + 512: (kc + 1) * 1024], start=st0, stop=sp0),
                    inc="pe" if sp0 else None)
            wv = [("pe", cmm[of]), ("qd", cxld)]
            if of >= 2:
                wv += [("qd", x2w[of - 2]), ("pe", cstat[of - 2]), ("ac", csq[of - 2])]
            x2_ve[of] = S.emit("vector", lambda v, par=par, of=of: v.scalar_tensor_tensor(
                X2S[par][:, :], P[par][:, :], outb[:, of:of + 1], XLD[par][:, :],
                ALU.add, ALU.add), inc="ve", waits=wv)
            x2w[of] = S.dma("sync", lambda sy, par=par, of=of: sy.dma_start(
                out=x2_sp[of * 128:(of + 1) * 128, :], in_=X2S[par][:, :]),
                waits=[("ve", x2_ve[of])])
            wsq = [("ve", x2_ve[of])]
            if of >= 1:
                wsq.append(("pe", cstat[of - 1]))
            csq[of] = S.emit("scalar", lambda s, par=par: s.activation(
                SQF[:, :], X2S[par][:, :], AF.Square), inc="ac", waits=wsq)
            st0, sp0 = (of == 0), (of == 31)
            S.emit("tensor", lambda t, par=par, st0=st0, sp0=sp0: t.matmul(
                P[2][0:1, 0:512], ones_cf[:, :], X2S[par][:, 0:512], start=st0, stop=sp0),
                waits=[("ve", x2_ve[of]), ("ac", csq[of])])
            S.emit("tensor", lambda t, par=par, st0=st0, sp0=sp0: t.matmul(
                P[2][0:1, 512:1024], ones_cf[:, :], X2S[par][:, 512:1024], start=st0, stop=sp0))
            S.emit("tensor", lambda t, st0=st0, sp0=sp0: t.matmul(
                P[3][0:1, 0:512], ones_cf[:, :], SQF[:, 0:512], start=st0, stop=sp0))
            cstat[of] = S.emit("tensor", lambda t, st0=st0, sp0=sp0: t.matmul(
                P[3][0:1, 512:1024], ones_cf[:, :], SQF[:, 512:1024], start=st0, stop=sp0),
                inc="pe")
        if debug:
            S.dma("sync", lambda sy: sy.dma_start(out=dbg["d_x2"], in_=x2_sp),
                  waits=[("qd", x2w[31]), ("ve", x2_ve[31])])

        bcast2 = emit_ln_rows(cstat[31])

        ln2_ve = {}
        for kc in range(32):
            sl = kc % 2
            w = [("pe", bcast2), ("qd", x2w[kc])]
            if kc >= 2:
                w.append(("ve", ln2_ve[kc - 2]))
            else:
                w.append(("ve", x2_ve[31]))
            xq2 = S.dma("sync", lambda sy, kc=kc, sl=sl: sy.dma_start(
                out=XLD[sl][:, :], in_=x2_sp[kc * 128:(kc + 1) * 128, :]), waits=w)
            S.emit("vector", lambda v, sl=sl: v.tensor_mul(
                SQF[:, :], XLD[sl][:, :], P[2][:, :]),
                waits=[("qd", xq2), ("pe", bcast2), ("ac", csq[31])])
            ln2_ve[kc] = S.emit("vector", lambda v, kc=kc: v.tensor_sub(
                CTX[:, kc * 1024:(kc + 1) * 1024], SQF[:, :], P[3][:, :]), inc="ve")
        ln2_done = ln2_ve[31]
        if debug:
            S.dma("sync", lambda sy: sy.dma_start(out=dbg["d_ln2"], in_=CTX[:, :]),
                  waits=[("ve", ln2_done)])

        aat_gd = S.dma("gpsimd", lambda g: g.dma_start(
            out=AAT[:, :].rearrange("p (e n) -> p e n", e=2),
            in_=a_allt_t.rearrange("e p n -> p e n")),
            waits=[("pe", b_end_pe), ("ve", ln2_done)])
        for e2 in range(2):
            bfr_gd = S.dma("gpsimd", lambda g, e2=e2: g.dma_start(
                out=BFR[e2][:, :], in_=bfr_t[e2, :, :]))
        wv16 = S.emit("vector", lambda v: v.tensor_mul(
            w16_sb[:, :], cf_sb[:, :], sc_sb[:, :]), inc="ve",
            waits=[("qd", pre_qd), ("ve", ln2_done)])
        wer_ac = {}
        for e2 in range(2):
            ww = [("ve", wv16)]
            if e2 == 1:
                ww.append(("ac", wer_ac[0]))
            wm = S.emit("tensor", lambda t, e2=e2: t.matmul(
                P[3][:, e2:e2 + 1], oh_sb[:, e2 * 128:(e2 + 1) * 128], w16_sb[:, :],
                start=True, stop=True), inc="pe", waits=ww)
            wer_ac[e2] = S.emit("scalar", lambda s, e2=e2: s.activation(
                wer_sb[:, e2:e2 + 1], P[3][:, e2:e2 + 1], AF.Identity), inc="ac",
                waits=[("pe", wm)])
        xa_ac = {}
        for e2 in range(2):
            w0 = [("gd", aat_gd), ("ve", ln2_done)]
            if e2 == 1:
                w0.append(("ac", xa_ac[0]))
            xam = None
            for kc in range(32):
                st0, sp0 = (kc == 0), (kc == 31)
                S.emit("tensor", lambda t, e2=e2, kc=kc, st0=st0, sp0=sp0: t.matmul(
                    P[2][:, 0:512], AAT[:, (e2 * 32 + kc) * 128:(e2 * 32 + kc + 1) * 128],
                    CTX[:, kc * 1024: kc * 1024 + 512], start=st0, stop=sp0),
                    waits=(w0 if kc == 0 else []))
                xam = S.emit("tensor", lambda t, e2=e2, kc=kc, st0=st0, sp0=sp0: t.matmul(
                    P[2][:, 512:1024], AAT[:, (e2 * 32 + kc) * 128:(e2 * 32 + kc + 1) * 128],
                    CTX[:, kc * 1024 + 512: (kc + 1) * 1024], start=st0, stop=sp0),
                    inc="pe" if sp0 else None)
            xa_ac[e2] = S.emit("scalar", lambda s, e2=e2: s.activation(
                XA[e2][:, :], P[2][:, :], AF.Identity, scale=wer_sb[:, e2:e2 + 1]),
                inc="ac", waits=[("pe", xam), ("ac", wer_ac[e2])])

        dmm = {}
        dout_ve = {}
        outw = {}
        for of in range(32):
            par = of % 2
            wfree = [("pe", dmm[of - 2])] if of >= 2 else [("pe", cmm[30 + of])]
            dwld = S.dma("gpsimd", lambda g, of=of, par=par: g.dma_start(
                out=WST[par][:, :], in_=ffn_wtt[of, :, :]), waits=wfree)
            wx = [("qd", x2w[of])]
            if of >= 2:
                wx.append(("ve", dout_ve[of - 2]))
            else:
                wx.append(("ve", ln2_done))
            dxld = S.dma("sync", lambda sy, of=of, par=par: sy.dma_start(
                out=XLD[par][:, :], in_=x2_sp[of * 128:(of + 1) * 128, :]), waits=wx)
            w0 = [("gd", dwld), ("ac", xa_ac[1]), ("ve", ln2_done)]
            if of >= 2:
                w0.append(("ve", dout_ve[of - 2]))
            for kc in range(32):
                st0 = (kc == 0)
                S.emit("tensor", lambda t, par=par, kc=kc, st0=st0: t.matmul(
                    P[par][:, 0:512], WST[par][:, kc * 128:(kc + 1) * 128],
                    CTX[:, kc * 1024: kc * 1024 + 512], start=st0, stop=False),
                    waits=(w0 if kc == 0 else []))
                S.emit("tensor", lambda t, par=par, kc=kc, st0=st0: t.matmul(
                    P[par][:, 512:1024], WST[par][:, kc * 128:(kc + 1) * 128],
                    CTX[:, kc * 1024 + 512: (kc + 1) * 1024], start=st0, stop=False))
            for e2 in range(2):
                sp0 = (e2 == 1)
                S.emit("tensor", lambda t, par=par, e2=e2, of=of, sp0=sp0: t.matmul(
                    P[par][:, 0:512], BFR[e2][:, of * 128:(of + 1) * 128],
                    XA[e2][:, 0:512], start=False, stop=sp0))
                dmm[of] = S.emit("tensor", lambda t, par=par, e2=e2, of=of, sp0=sp0: t.matmul(
                    P[par][:, 512:1024], BFR[e2][:, of * 128:(of + 1) * 128],
                    XA[e2][:, 512:1024], start=False, stop=sp0),
                    inc="pe" if sp0 else None)
            wv = [("pe", dmm[of]), ("qd", dxld)]
            if of >= 2:
                wv.append(("qd", outw[of - 2]))
            dout_ve[of] = S.emit("vector", lambda v, par=par, of=of: v.scalar_tensor_tensor(
                X2S[par][:, :], P[par][:, :], ffnb[:, of:of + 1], XLD[par][:, :],
                ALU.add, ALU.add), inc="ve", waits=wv)
            outw[of] = S.dma("sync", lambda sy, par=par, of=of: sy.dma_start(
                out=out_fm[of * 128:(of + 1) * 128, :], in_=X2S[par][:, :]),
                waits=[("ve", dout_ve[of])])

        S.emit("sync", None, waits=[("qd", S.n["qd"])])

        block.tensor(lambda e: S.materialize(e, "tensor", sems))
        block.vector(lambda e: S.materialize(e, "vector", sems))
        block.scalar(lambda e: S.materialize(e, "scalar", sems))
        block.gpsimd(lambda e: S.materialize(e, "gpsimd", sems))
        block.sync(lambda e: S.materialize(e, "sync", sems))

    return nc



def _prep_inputs(inputs):
    x = np.asarray(inputs["L0_attn_hidden_states"], dtype=np.float32)[0]
    qkv_w = np.asarray(inputs["L0_attn_qkv_w"], dtype=np.float32)
    qkv_b = np.asarray(inputs["L0_attn_qkv_b"], dtype=np.float32)
    out_w = np.asarray(inputs["L0_attn_out_w"], dtype=np.float32)
    out_b = np.asarray(inputs["L0_attn_out_b"], dtype=np.float32)
    ffn_W = np.asarray(inputs["L0_ffn_W"], dtype=np.float32)
    ffn_b = np.asarray(inputs["L0_ffn_bias"], dtype=np.float32)
    A = np.asarray(inputs["L0_ffn_A"], dtype=np.float32)
    Bf = np.asarray(inputs["L0_ffn_Bf"], dtype=np.float32)
    cf = np.ascontiguousarray(np.asarray(inputs["L0_ffn_coeffs"], dtype=np.float32).reshape(16, 1))
    sc = np.ascontiguousarray(np.asarray(inputs["L0_ffn_scales"], dtype=np.float32).reshape(16, 1))

    C = np.ascontiguousarray
    qkv_wt = C(qkv_w.reshape(32, 128, 96, 128).transpose(2, 1, 0, 3).reshape(96, 128, 4096))
    out_wt = C(out_w.reshape(32, 128, 32, 128).transpose(2, 1, 0, 3).reshape(32, 128, 4096))
    ffn_wtt = C(ffn_W.T.reshape(32, 128, 32, 128).transpose(2, 1, 0, 3).reshape(32, 128, 4096))
    a_allt_t = C(A.reshape(256, 4096).T.reshape(32, 128, 2, 128)
                 .transpose(2, 1, 0, 3).reshape(2, 128, 4096))
    bfr_t = C(Bf.transpose(0, 2, 1).reshape(2, 128, 4096))
    qkvb_fm = C(qkv_b.reshape(96, 128).T)
    outb_fm = C(out_b.reshape(32, 128).T)
    ffnb_fm = C(ffn_b.reshape(32, 128).T)

    oh = np.zeros((16, 256), dtype=np.float32)
    for e2 in range(2):
        for p in range(128):
            oh[e2 * 8 + p // 16, e2 * 128 + p] = 1.0
    ident = np.eye(128, dtype=np.float32)
    swap_m = np.zeros((128, 128), dtype=np.float32)
    swap_m[np.arange(128), (np.arange(128) + 64) % 128] = 1.0
    inv = 10000.0 ** (-np.arange(64, dtype=np.float64) / 64.0)

    in_maps = []
    token_idx_all = []
    for c in range(NCORE):
        blocks = [8 * t + c for t in range(NT)]
        tok = np.concatenate([np.arange(b * 128, (b + 1) * 128) for b in blocks])
        token_idx_all.append(tok)
        ang = np.outer(inv, tok.astype(np.float64))
        cs = np.concatenate([np.cos(ang), np.cos(ang)], axis=0).astype(np.float32)
        csw = np.concatenate([np.sin(ang), -np.sin(ang)], axis=0).astype(np.float32)
        maskT = np.zeros((128, 1024), dtype=np.float32)
        for b in range(8):
            if b == c:
                maskT[:, b * 128:(b + 1) * 128] = np.where(
                    np.arange(128)[:, None] > np.arange(128)[None, :],
                    np.float32(MASKV), np.float32(0.0))
            elif b > c:
                maskT[:, b * 128:(b + 1) * 128] = MASKV
        in_maps.append({
            "x_fm": C(x[tok, :].T),
            "qkv_wt": qkv_wt, "qkvb_fm": qkvb_fm,
            "out_wt": out_wt, "outb_fm": outb_fm,
            "ffn_wtt": ffn_wtt, "ffnb_fm": ffnb_fm,
            "a_allt_t": a_allt_t, "bfr_t": bfr_t,
            "cf_in": cf, "sc_in": sc,
            "cs_in": cs, "csw_in": csw,
            "maskT_in": maskT, "id_in": ident, "oh_in": oh, "swap_in": swap_m,
        })
    return in_maps, token_idx_all


def kernel(**inputs):
    in_maps, token_idx_all = _prep_inputs(inputs)
    nc = build_program(debug=False)
    res = run_bass_kernel_spmd(nc, in_maps, core_ids=list(range(NCORE)))
    out = np.empty((SEQ, D), dtype=np.float32)
    for c in range(NCORE):
        out[token_idx_all[c], :] = res.results[c]["out_fm"].T
    return out.reshape(1, SEQ, D)


# version 06
# speedup vs baseline: 1.0418x; 1.0418x over previous
import math
from contextlib import ExitStack

import numpy as np

import concourse.bass as bass
import concourse.mybir as mybir
from concourse.bass_utils import run_bass_kernel_spmd

F32 = mybir.dt.float32
BF16 = mybir.dt.bfloat16
AF = mybir.ActivationFunctionType
ALU = mybir.AluOpType

D = 4096
SEQ = 8192
NCORE = 8
TPC = 1024
NT = 8
SCALE = 1.0 / math.sqrt(128.0)
MASKV = -66504.0
EPS = 1e-5
INV_D = 1.0 / 4096.0
L_SHIFT = float(2.0 ** 24)

ENGS = ("tensor", "vector", "scalar", "gpsimd", "sync")
OWN = {"tensor": "pe", "vector": "ve", "scalar": "ac", "gpsimd": "po"}


class Sched:

    def __init__(self):
        self.streams = {e: [] for e in ENGS}
        self.n = {s: 0 for s in ("pe", "ve", "ac", "po", "qd", "gd",
                                 "ck", "cv", "cn")}

    def emit(self, eng, fn, inc=None, inc_by=1, waits=()):
        ws = [(s, v) for (s, v) in waits if v and v > 0 and s != OWN.get(eng)]
        self.streams[eng].append((fn, inc, inc_by, ws))
        if inc:
            self.n[inc] += inc_by
            return self.n[inc]
        return None

    def dma(self, eng, fn, waits=()):
        q = "qd" if eng == "sync" else "gd"
        return self.emit(eng, fn, inc=q, inc_by=16, waits=waits)

    def snap(self):
        return dict(self.n)

    def materialize(self, eng_obj, eng_name, sems):
        seen = {}
        for fn, inc, inc_by, ws in self.streams[eng_name]:
            for s, v in ws:
                if seen.get(s, 0) < v:
                    eng_obj.wait_ge(sems[s], v)
                    seen[s] = v
            ins = fn(eng_obj) if fn is not None else None
            if inc:
                if ins is None:
                    ins = eng_obj.nop()
                ins.then_inc(sems[inc], inc_by)


def build_program(debug=False):
    nc = bass.Bass("TRN2", target_bir_lowering=False, debug=False)

    def din(name, shape):
        return nc.dram_tensor(name, shape, F32, kind="ExternalInput").ap()

    x_fm = din("x_fm", [D, TPC])
    qkv_wt = din("qkv_wt", [96, 128, 4096])
    qkvb_fm = din("qkvb_fm", [128, 96])
    out_wt = din("out_wt", [32, 128, 4096])
    outb_fm = din("outb_fm", [128, 32])
    ffn_wtt = din("ffn_wtt", [32, 128, 4096])
    ffnb_fm = din("ffnb_fm", [128, 32])
    a_allt_t = din("a_allt_t", [2, 128, 4096])
    bfr_t = din("bfr_t", [2, 128, 4096])
    cf_in = din("cf_in", [16, 1])
    sc_in = din("sc_in", [16, 1])
    cs_in = din("cs_in", [128, TPC])
    csw_in = din("csw_in", [128, TPC])
    swap_in = din("swap_in", [128, 128])
    maskT_in = din("maskT_in", [128, 1024])
    id_in = din("id_in", [128, 128])
    oh_in = din("oh_in", [16, 256])

    out_fm = nc.dram_tensor("out_fm", [D, TPC], F32, kind="ExternalOutput").ap()

    def dint(name, shape, dt=BF16):
        return nc.dram_tensor(name, shape, dt, kind="Internal").ap()

    q_spill = dint("q_spill", [D, TPC])
    k_in = dint("k_in", [D, TPC])
    v_in = dint("v_in", [TPC, D])
    k_all = dint("k_all", [NCORE * D, TPC])
    v_all = dint("v_all", [SEQ, D])
    kn_i = dint("kn_i", [1, 32], F32)
    knm_i = dint("knm_i", [1, 32], F32)
    x2_sp = dint("x2_sp", [D, TPC], F32)

    dbg = {}
    if debug:
        for nm, shape, dt in [
            ("d_ln1", [128, 32 * 1024], BF16), ("d_ctx", [128, 32 * 1024], BF16),
            ("d_kall", [NCORE * D, TPC], BF16), ("d_vall", [SEQ, D], BF16),
            ("d_qsp", [D, TPC], BF16), ("d_bias", [128, 32], F32),
            ("d_x2", [D, TPC], F32), ("d_ln2", [128, 32 * 1024], BF16),
        ]:
            dbg[nm] = nc.dram_tensor(nm, shape, dt, kind="ExternalOutput").ap()

    with ExitStack() as stk:
        E = stk.enter_context
        AB = E(nc.sbuf_tensor("AB", [128, 49152], BF16))
        CTX = E(nc.sbuf_tensor("CTX", [128, 32768], BF16))
        AFR = E(nc.sbuf_tensor("AFR", [128, 3072], F32))
        cs_sb = E(nc.sbuf_tensor("cs_sb", [128, TPC], F32))
        csw_sb = E(nc.sbuf_tensor("csw_sb", [128, TPC], F32))
        maskT = E(nc.sbuf_tensor("maskT", [128, 1024], BF16))
        id_sb = E(nc.sbuf_tensor("id_sb", [128, 128], BF16))
        swp_sb = E(nc.sbuf_tensor("swp_sb", [128, 128], BF16))
        qkvb = E(nc.sbuf_tensor("qkvb", [128, 96], F32))
        outb = E(nc.sbuf_tensor("outb", [128, 32], F32))
        ffnb = E(nc.sbuf_tensor("ffnb", [128, 32], F32))
        bias_sb = E(nc.sbuf_tensor("bias_sb", [128, 32], F32))
        wer_sb = E(nc.sbuf_tensor("wer_sb", [128, 2], F32))
        oh_sb = E(nc.sbuf_tensor("oh_sb", [16, 256], F32))
        cf_sb = E(nc.sbuf_tensor("cf_sb", [16, 1], F32))
        sc_sb = E(nc.sbuf_tensor("sc_sb", [16, 1], F32))
        w16_sb = E(nc.sbuf_tensor("w16_sb", [16, 1], F32))
        ones_cf = E(nc.sbuf_tensor("ones_cf", [128, 1], F32))
        ones_cb = E(nc.sbuf_tensor("ones_cb", [128, 1], BF16))
        ones_rf = E(nc.sbuf_tensor("ones_rf", [1, 128], F32))
        ROWS = E(nc.sbuf_tensor("ROWS", [1, 4096], F32))
        ROW2 = E(nc.sbuf_tensor("ROW2", [1, 256], F32))
        CST = E(nc.sbuf_tensor("CST", [1, 8], F32))

        P = [E(nc.psum_tensor(f"P{i}", [128, 1024], F32)) for i in range(4)]
        P2bf = P[2].ap().bitcast(BF16)

        sems = {}
        for s in ("pe", "ve", "ac", "po", "qd", "gd", "ck", "cv", "cn"):
            sems[s] = E(nc.semaphore(s + "_s"))
        block = E(nc.Block())

        S = Sched()

        r_mean = ROWS[0:1, 0:1024]
        r_tmp = ROWS[0:1, 1024:2048]
        r_rstd = ROWS[0:1, 2048:3072]
        r_brow = ROWS[0:1, 3072:4096]
        r_kn = ROW2[0:1, 0:32]
        r_knm = ROW2[0:1, 32:64]
        r_qn = ROW2[0:1, 64:96]
        r_b1 = ROW2[0:1, 96:128]
        r_b2 = ROW2[0:1, 128:160]
        r_b3 = ROW2[0:1, 160:192]

        def ab(c0, w):
            return AB[:, c0:c0 + w]

        def afr(c0, w):
            return AFR[:, c0:c0 + w]

        LN1 = lambda kc: ab(kc * 1024, 1024)
        WST = [ab(32768, 4096), ab(36864, 4096)]
        ROPE = [ab(40960, 1024), ab(41984, 1024)]
        VSTG = [ab(43008, 1024), ab(44032, 1024)]
        VTM = [ab(45056, 1024), ab(46080, 1024)]
        SQK = ab(47104, 1024)
        KB = [ab(0, 8192), ab(8192, 8192)]
        VB = [ab(16384, 8192), ab(24576, 8192), ab(32768, 8192)]
        QB = [ab(40960, 1024), ab(41984, 1024)]
        PT = [ab(43008 + i * 1024, 1024) for i in range(4)]
        XA = [ab(0, 1024), ab(1024, 1024)]
        BFR = [ab(2048, 4096), ab(6144, 4096)]
        AAT = ab(10240, 8192)

        XLD = [afr(0, 1024), afr(1024, 1024)]
        TA = XLD[0]
        TB = XLD[1]
        SQF = afr(2048, 1024)
        RLB = afr(2048, 1024)
        X2S = XLD

        for dst, src in ((cs_sb, cs_in), (csw_sb, csw_in), (qkvb, qkvb_fm),
                         (outb, outb_fm), (ffnb, ffnb_fm), (oh_sb, oh_in),
                         (cf_sb, cf_in), (sc_sb, sc_in)):
            S.dma("sync", lambda sy, d=dst, s=src: sy.dma_start(out=d[:, :], in_=s))
        pre_qd = S.n["qd"]
        for dst, src in ((maskT, maskT_in), (id_sb, id_in), (swp_sb, swap_in)):
            S.dma("gpsimd", lambda g, d=dst, s=src: g.dma_start(out=d[:, :], in_=s))
        S.emit("gpsimd", lambda g: g.memset(CST[0:1, 0:1], EPS))
        S.emit("gpsimd", lambda g: g.memset(CST[0:1, 1:2], SCALE))
        nc.const_aps.register(F32, EPS, CST[0:1, 0:1])
        nc.const_aps.register(F32, SCALE, CST[0:1, 1:2])
        S.emit("gpsimd", lambda g: g.memset(ones_cf[:, :], 1.0))
        S.emit("gpsimd", lambda g: g.memset(ones_cb[:, :], 1.0))
        pre_po = S.emit("gpsimd", lambda g: g.memset(ones_rf[:, :], 1.0), inc="po")

        xq = {}
        stat_pe = {}
        sq_ac = {}
        for c in range(32):
            sl = c % 2
            w = [("pe", stat_pe[c - 2]), ("ac", sq_ac[c - 2])] if c >= 2 else []
            xq[c] = S.dma("sync", lambda sy, c=c, sl=sl: sy.dma_start(
                out=XLD[sl][:, :], in_=x_fm[c * 128:(c + 1) * 128, :]), waits=w)
            wsq = [("qd", xq[c])]
            if c >= 1:
                wsq.append(("pe", stat_pe[c - 1]))
            if c == 0:
                wsq.append(("po", pre_po))
            sq_ac[c] = S.emit("scalar", lambda s, sl=sl: s.activation(
                SQF[:, :], XLD[sl][:, :], AF.Square), inc="ac", waits=wsq)
            st0, sp0 = (c == 0), (c == 31)
            S.emit("tensor", lambda t, sl=sl, st0=st0, sp0=sp0: t.matmul(
                P[2][0:1, 0:512], ones_cf[:, :], XLD[sl][:, 0:512], start=st0, stop=sp0),
                waits=[("qd", xq[c]), ("po", pre_po)])
            S.emit("tensor", lambda t, sl=sl, st0=st0, sp0=sp0: t.matmul(
                P[2][0:1, 512:1024], ones_cf[:, :], XLD[sl][:, 512:1024], start=st0, stop=sp0))
            S.emit("tensor", lambda t, st0=st0, sp0=sp0: t.matmul(
                P[3][0:1, 0:512], ones_cf[:, :], SQF[:, 0:512], start=st0, stop=sp0),
                waits=[("ac", sq_ac[c])])
            stat_pe[c] = S.emit("tensor", lambda t, st0=st0, sp0=sp0: t.matmul(
                P[3][0:1, 512:1024], ones_cf[:, :], SQF[:, 512:1024], start=st0, stop=sp0),
                inc="pe")

        def emit_ln_rows(stat_wait):
            S.emit("vector", lambda v: v.tensor_scalar_mul(
                r_mean[:, :], P[2][0:1, 0:1024], INV_D), waits=[("pe", stat_wait)])
            S.emit("vector", lambda v: v.tensor_mul(r_tmp[:, :], r_mean[:, :], r_mean[:, :]))
            vv = S.emit("vector", lambda v: v.scalar_tensor_tensor(
                r_brow[:, :], P[3][0:1, 0:1024], INV_D, r_tmp[:, :], ALU.mult, ALU.subtract),
                inc="ve")
            aa = S.emit("scalar", lambda s: s.activation(
                r_tmp[:, :], r_brow[:, :], AF.Sqrt, bias=EPS), inc="ac", waits=[("ve", vv)])
            S.emit("vector", lambda v: v.reciprocal(r_rstd[:, :], r_tmp[:, :]),
                   waits=[("ac", aa)])
            rw = S.emit("vector", lambda v: v.tensor_mul(
                r_brow[:, :], r_mean[:, :], r_rstd[:, :]), inc="ve")
            S.emit("tensor", lambda t: t.matmul(
                P[2][:, 0:512], ones_rf[:, :], r_rstd[:, 0:512], start=True, stop=True),
                waits=[("ve", rw), ("po", pre_po)])
            S.emit("tensor", lambda t: t.matmul(
                P[2][:, 512:1024], ones_rf[:, :], r_rstd[:, 512:1024], start=True, stop=True))
            S.emit("tensor", lambda t: t.matmul(
                P[3][:, 0:512], ones_rf[:, :], r_brow[:, 0:512], start=True, stop=True))
            return S.emit("tensor", lambda t: t.matmul(
                P[3][:, 512:1024], ones_rf[:, :], r_brow[:, 512:1024], start=True, stop=True),
                inc="pe")

        bcast_pe = emit_ln_rows(stat_pe[31])

        ln1_ve = {}
        for c in range(32):
            sl = c % 2
            w = [("pe", bcast_pe)]
            if c >= 2:
                w.append(("ve", ln1_ve[c - 2]))
            xq2 = S.dma("sync", lambda sy, c=c, sl=sl: sy.dma_start(
                out=XLD[sl][:, :], in_=x_fm[c * 128:(c + 1) * 128, :]), waits=w)
            S.emit("vector", lambda v, sl=sl: v.tensor_mul(
                SQF[:, :], XLD[sl][:, :], P[2][:, :]),
                waits=[("qd", xq2), ("pe", bcast_pe)])
            ln1_ve[c] = S.emit("vector", lambda v, c=c: v.tensor_sub(
                LN1(c)[:, :], SQF[:, :], P[3][:, :]), inc="ve")
        ln1_done = ln1_ve[31]
        if debug:
            S.dma("sync", lambda sy: sy.dma_start(out=dbg["d_ln1"], in_=AB[:, 0:32768]),
                  waits=[("ve", ln1_done)])

        qkv_mm = {}
        chunk_free = {}
        last_rope_dma = {}
        last_sqk = {"mm": 0, "rd": ln1_done}
        last_rope_mm = {"pe": 0, "ve": 0, "evac": 0}
        wld = {}

        CTXF = CTX.ap().bitcast(F32)
        WSTG = [CTXF[:, i * 4096:(i + 1) * 4096] for i in range(4)]
        wcast = {}

        def emit_wload(idx, c):
            wfree = [("ac", wcast[idx - 4])] if idx >= 4 else []
            wld[idx] = S.dma("sync", lambda sy, c=c, sl=idx % 4: sy.dma_start(
                out=WSTG[sl][:, :], in_=qkv_wt[c, :, :]), waits=wfree)

        def emit_wcast(idx):
            wc = [("qd", wld[idx])]
            if idx >= 2:
                wc.append(("pe", qkv_mm[idx - 2]))
            wcast[idx] = S.emit("scalar", lambda sc, sl4=idx % 4, sl=idx % 2:
                                sc.activation(WST[sl][:, :], WSTG[sl4][:, :],
                                              AF.Identity), inc="ac", waits=wc)

        def emit_wmms(idx):
            psl = idx % 2
            w0 = [("ac", wcast[idx]), ("ve", ln1_done)]
            if idx >= 2:
                w0.append(chunk_free[idx - 2])
            last = None
            for kc in range(32):
                st0, sp0 = (kc == 0), (kc == 31)
                S.emit("tensor", lambda t, sl=psl, kc=kc, st0=st0, sp0=sp0:
                       t.matmul(P[sl][:, 0:512], WST[sl][:, kc * 128:(kc + 1) * 128],
                                LN1(kc)[:, 0:512], start=st0, stop=sp0),
                       waits=(w0 if kc == 0 else []))
                last = S.emit("tensor", lambda t, sl=psl, kc=kc, st0=st0, sp0=sp0:
                              t.matmul(P[sl][:, 512:1024], WST[sl][:, kc * 128:(kc + 1) * 128],
                                       LN1(kc)[:, 512:1024], start=st0, stop=sp0),
                              inc="pe" if sp0 else None)
            qkv_mm[idx] = last

        def emit_rope_a(idx, c):
            psl = idx % 2
            w = [("pe", qkv_mm[idx]), ("pe", last_rope_mm["pe"])]
            S.emit("vector", lambda v, psl=psl, c=c: v.scalar_tensor_tensor(
                VSTG[0][:, :], P[psl][:, :], qkvb[:, c:c + 1], cs_sb[:, :],
                ALU.add, ALU.mult), waits=w)
            rv = S.emit("vector", lambda v, psl=psl, c=c: v.scalar_tensor_tensor(
                VSTG[1][:, :], P[psl][:, :], qkvb[:, c:c + 1], csw_sb[:, :],
                ALU.add, ALU.mult), inc="ve")
            chunk_free[idx] = ("ve", rv)
            last_rope_mm["ve"] = rv

        def emit_rope_b(idx, h, dest_dram, nrow):
            sl = idx % 2
            wmm = [("ve", last_rope_mm["ve"]), ("ac", last_rope_mm["evac"])]
            S.emit("tensor", lambda t: t.matmul(
                P[3][:, 0:512], id_sb[:, :], VSTG[0][:, 0:512], start=True, stop=False),
                waits=wmm)
            S.emit("tensor", lambda t: t.matmul(
                P[3][:, 0:512], swp_sb[:, :], VSTG[1][:, 0:512], start=False, stop=True))
            S.emit("tensor", lambda t: t.matmul(
                P[3][:, 512:1024], id_sb[:, :], VSTG[0][:, 512:1024], start=True, stop=False))
            rmm = S.emit("tensor", lambda t: t.matmul(
                P[3][:, 512:1024], swp_sb[:, :], VSTG[1][:, 512:1024], start=False, stop=True),
                inc="pe")
            last_rope_mm["pe"] = rmm
            wev = [("pe", rmm)]
            if sl in last_rope_dma:
                wev.append(("qd", last_rope_dma[sl]))
            ev = S.emit("scalar", lambda s, sl=sl: s.activation(
                ROPE[sl][:, :], P[3][:, :], AF.Identity), inc="ac", waits=wev)
            last_rope_mm["evac"] = ev
            last_rope_dma[sl] = S.dma("sync", lambda sy, sl=sl, h=h, dd=dest_dram:
                                      sy.dma_start(out=dd[h * 128:(h + 1) * 128, :],
                                                   in_=ROPE[sl][:, :]),
                                      waits=[("ac", ev)])
            wsq = [("ac", ev), ("pe", last_sqk["mm"])]
            sq = S.emit("scalar", lambda s, sl=sl: s.activation(
                SQK[:, :], ROPE[sl][:, :], AF.Square), inc="ac", waits=wsq)
            last_rope_dma[(sl, "sq")] = sq
            wmm2 = [("ac", sq), ("ve", last_sqk["rd"])]
            S.emit("tensor", lambda t: t.matmul(
                P[2][0:1, 0:512], ones_cb[:, :], SQK[:, 0:512], start=True, stop=True),
                waits=wmm2)
            last_sqk["mm"] = S.emit("tensor", lambda t: t.matmul(
                P[2][0:1, 512:1024], ones_cb[:, :], SQK[:, 512:1024], start=True, stop=True),
                inc="pe")
            last_sqk["rd"] = S.emit("vector", lambda v, h=h, nrow=nrow: v.reduce_max(
                nrow[:, h:h + 1], P[2][0:1, 0:1024], axis=mybir.AxisListType.X),
                inc="ve", waits=[("pe", last_sqk["mm"])])

        qkvcol = lambda idx: (32 + idx) if idx < 32 else ((64 + idx - 32) if idx < 64 else idx - 64)
        emit_wload(0, qkvcol(0))
        emit_wload(1, qkvcol(1))
        emit_wcast(0)
        for i in range(32):
            if i + 2 < 96:
                emit_wload(i + 2, qkvcol(i + 2))
            emit_wmms(i)
            if i >= 1:
                emit_rope_b(i - 1, i - 1, k_in, r_kn)
            emit_rope_a(i, 32 + i)
            emit_wcast(i + 1)
        emit_rope_b(31, 31, k_in, r_kn)
        k_done_qd = last_rope_dma[1]
        kn_qd = S.dma("sync", lambda sy: sy.dma_start(out=kn_i, in_=r_kn),
                      waits=[("ve", last_sqk["rd"])])
        k_nrm_rd = last_sqk["rd"]

        vtm_dma = {}
        vtm_ac = {}

        def emit_v_evac(i):
            idx = 32 + i
            h = i
            psl = idx % 2
            ve_ac = S.emit("scalar", lambda s, psl=psl, c=64 + i: s.activation(
                VSTG[psl][:, :], P[psl][:, :], AF.Identity, bias=qkvb[:, c:c + 1]),
                inc="ac", waits=[("pe", qkv_mm[idx])])
            chunk_free[idx] = ("ac", ve_ac)
            wtr = [("ac", ve_ac)]
            if i == 0:
                wtr.append(("ve", k_nrm_rd))
            if i >= 1:
                wtr.append(("ac", vtm_ac[i - 1]))
            last = None
            for j in range(8):
                last = S.emit("tensor", lambda t, psl=psl, j=j: t.transpose(
                    P2bf[:, j * 128:(j + 1) * 128], VSTG[psl][:, j * 128:(j + 1) * 128],
                    id_sb[:, :]), inc="pe" if j == 7 else None,
                    waits=(wtr if j == 0 else []))
            wvt = [("pe", last)]
            if i >= 2:
                wvt.append(("qd", vtm_dma[i - 2]))
            vtm_ac[i] = S.emit("scalar", lambda s, psl=psl: s.activation(
                VTM[psl][:, :], P2bf[:, 0:1024], AF.Identity), inc="ac", waits=wvt)
            vtm_dma[i] = S.dma("sync", lambda sy, psl=psl, h=h: sy.dma_start(
                out=v_in.rearrange("(j p) o -> p j o", p=128)[:, :, h * 128:(h + 1) * 128],
                in_=VTM[psl][:, :].rearrange("p (j o) -> p j o", j=8)),
                waits=[("ac", vtm_ac[i])])

        for i in range(32):
            idx = 32 + i
            if idx + 2 < 96:
                emit_wload(idx + 2, qkvcol(idx + 2))
            emit_wmms(idx)
            if i >= 1:
                emit_v_evac(i - 1)
            emit_wcast(idx + 1)
        emit_v_evac(31)
        v_done_qd = vtm_dma[31]

        for i in range(32):
            idx = 64 + i
            if idx + 2 < 96:
                emit_wload(idx + 2, qkvcol(idx + 2))
            emit_wmms(idx)
            if i >= 1:
                emit_rope_b(idx - 1, i - 1, q_spill, r_qn)
            emit_rope_a(idx, i)
            if idx + 1 < 96:
                emit_wcast(idx + 1)
        emit_rope_b(95, 31, q_spill, r_qn)
        q_nrm_rd = last_sqk["rd"]

        S.emit("gpsimd", lambda g: g.collective_compute(
            "AllGather", ALU.bypass, replica_groups=[list(range(NCORE))],
            ins=[k_in.tensor.ap().opt()], outs=[k_all.tensor.ap().opt()]),
            inc="ck", waits=[("qd", k_done_qd), ("pe", S.n["pe"]),
                             ("ve", S.n["ve"]), ("ac", S.n["ac"]),
                             ("qd", S.n["qd"])])
        S.emit("gpsimd", lambda g: g.collective_compute(
            "AllGather", ALU.bypass, replica_groups=[list(range(NCORE))],
            ins=[v_in.tensor.ap().opt()], outs=[v_all.tensor.ap().opt()]),
            inc="cv")
        S.emit("gpsimd", lambda g: g.collective_compute(
            "AllReduce", ALU.max, replica_groups=[list(range(NCORE))],
            ins=[kn_i.tensor.ap().opt()], outs=[knm_i.tensor.ap().opt()]),
            inc="cn")





        knm_qd = S.dma("sync", lambda sy: sy.dma_start(
            out=r_knm, in_=knm_i), waits=[("cn", 1)])
        bb = S.emit("vector", lambda v: v.tensor_mul(
            r_b1[:, :], r_qn[:, :], r_knm[:, :]),
            inc="ve", waits=[("qd", knm_qd), ("ve", q_nrm_rd)])
        bs = S.emit("scalar", lambda s: s.activation(
            r_b2[:, :], r_b1[:, :], AF.Sqrt), inc="ac", waits=[("ve", bb)])
        bv = S.emit("vector", lambda v: v.tensor_scalar_mul(
            r_b3[:, :], r_b2[:, :], -SCALE), inc="ve", waits=[("ac", bs)])
        bm = S.emit("tensor", lambda t: t.matmul(
            P[2][:, 0:32], ones_rf[:, :], r_b3[:, :], start=True, stop=True),
            inc="pe", waits=[("ve", bv), ("ve", q_nrm_rd)])
        bias_ac = S.emit("scalar", lambda s: s.activation(
            bias_sb[:, :], P[2][:, 0:32], AF.Identity), inc="ac", waits=[("pe", bm)])
        if debug:
            S.dma("sync", lambda sy: sy.dma_start(out=dbg["d_bias"], in_=bias_sb[:, :]),
                  waits=[("ac", bias_ac)])
            S.dma("sync", lambda sy: sy.dma_start(out=dbg["d_qsp"], in_=q_spill),
                  waits=[("qd", last_rope_dma[1])])
            S.dma("sync", lambda sy: sy.dma_start(out=dbg["d_kall"], in_=k_all),
                  waits=[("ck", 1)])
            S.dma("sync", lambda sy: sy.dma_start(out=dbg["d_vall"], in_=v_all),
                  waits=[("cv", 1)])

        a_end = S.snap()

        kq_qd = {}
        v_qd = {}
        half_last_pv = {}
        exp_ac = {}
        mask_ve = {}
        pv_pe = {}
        ctx_ve = {}
        rlb_ac = {}
        head_last_pe = {}

        for p16 in range(16):
            for hh in range(2):
                h = 2 * p16 + hh
                wk = [("ck", 1), ("cv", 1), ("pe", a_end["pe"]),
                      ("ve", a_end["ve"]), ("ac", a_end["ac"])] if h < 2 else \
                     [("pe", head_last_pe[h - 2])]
                for b in range(8):
                    S.dma("sync", lambda sy, h=h, b=b: sy.dma_start(
                        out=KB[h % 2][:, b * 1024:(b + 1) * 1024],
                        in_=k_all[b * D + h * 128: b * D + (h + 1) * 128, :]),
                        waits=(wk if b == 0 else []))
                kq_qd[h] = S.dma("sync", lambda sy, h=h: sy.dma_start(
                    out=QB[h % 2][:, :], in_=q_spill[h * 128:(h + 1) * 128, :]))
            for a in range(8):
                g = 2 * p16 + (a // 4)
                vw = [("cv", 1)]
                if g < 2:
                    vw += [("pe", a_end["pe"]), ("ve", a_end["ve"]), ("ac", a_end["ac"])]
                else:
                    vw.append(("pe", half_last_pv.get(g - 3, 0)))
                v_qd[(p16, a)] = S.dma("sync", lambda sy, p16=p16, a=a, sl=(2 * p16 + a // 4) % 3:
                                       sy.dma_start(
                    out=VB[sl][:, (a % 4) * 2048:(a % 4 + 1) * 2048]
                        .rearrange("p (b c) -> p b c", b=8),
                    in_=v_all.rearrange("(b a p) o -> b a p o", a=8, p=128)
                        [:, a, :, p16 * 256:(p16 + 1) * 256].rearrange("b p o -> p b o")),
                    waits=vw)

            for hh in range(2):
                h = 2 * p16 + hh
                qb = QB[h % 2]
                kb = KB[h % 2]

                def emit_qk(h, hh, p16, a, b, qb=qb, kb=kb):
                    gidx = h * 64 + a * 8 + b
                    par = gidx % 2
                    suf = (8 - a) * 128
                    rstar = 512 - a * 128
                    s0 = P[0][:, par * 512:(par + 1) * 512]
                    s1 = P[1][:, par * 512:(par + 1) * 512]
                    kt = b * 1024 + a * 128
                    slot = gidx % 4
                    wqk = [("qd", kq_qd[h])]
                    if gidx >= 2:
                        wqk.append(("ac", exp_ac[gidx - 2]))
                    else:
                        wqk += [("ve", a_end["ve"]), ("ac", a_end["ac"])]
                    if h > 0 and a == 0 and b == 0:
                        wqk += [("ve", ctx_ve[h - 1]), ("ac", rlb_ac[h - 1])]
                    if a <= 3:
                        S.emit("tensor", lambda t, s0=s0, kt=kt, a=a, rs=rstar:
                               t.matmul(s0[:, 0:rs], kb[:, kt:kt + 128],
                                        qb[:, a * 128:512], start=True, stop=False),
                               waits=wqk)
                        S.emit("tensor", lambda t, s1=s1, kt=kt:
                               t.matmul(s1[:, 0:512], kb[:, kt:kt + 128],
                                        qb[:, 512:1024], start=True, stop=True))
                        qk_pe = S.emit("tensor", lambda t, s0=s0, b=b: t.matmul(
                            s0[:, 0:128], id_sb[:, :], maskT[:, b * 128:(b + 1) * 128],
                            start=False, stop=True), inc="pe")
                    else:
                        S.emit("tensor", lambda t, s0=s0, kt=kt, a=a, suf=suf:
                               t.matmul(s0[:, 0:suf], kb[:, kt:kt + 128],
                                        qb[:, a * 128:1024], start=True, stop=False),
                               waits=wqk)
                        qk_pe = S.emit("tensor", lambda t, s0=s0, b=b: t.matmul(
                            s0[:, 0:128], id_sb[:, :], maskT[:, b * 128:(b + 1) * 128],
                            start=False, stop=True), inc="pe")
                    we = [("pe", qk_pe)]
                    if gidx >= 4:
                        we.append(("pe", pv_pe.get(gidx - 4, 0)))
                    else:
                        we.append(("qd", v_done_qd))
                    if a <= 3:
                        S.emit("scalar", lambda sc, s0=s0, slot=slot, h=h, rs=rstar:
                               sc.activation(PT[slot][:, 0:rs], s0[:, 0:rs], AF.Exp,
                                             bias=bias_sb[:, h:h + 1], scale=SCALE),
                               waits=we)
                        exp_ac[gidx] = S.emit(
                            "scalar", lambda sc, s1=s1, slot=slot, h=h, rs=rstar, suf=suf:
                            sc.activation(PT[slot][:, rs:suf], s1[:, 0:512], AF.Exp,
                                          bias=bias_sb[:, h:h + 1], scale=SCALE),
                            inc="ac")
                    else:
                        exp_ac[gidx] = S.emit(
                            "scalar", lambda sc, s0=s0, slot=slot, h=h, suf=suf:
                            sc.activation(PT[slot][:, 0:suf], s0[:, 0:suf], AF.Exp,
                                          bias=bias_sb[:, h:h + 1], scale=SCALE),
                            inc="ac", waits=we)

                def emit_lpv(h, hh, p16, a, b):
                    gidx = h * 64 + a * 8 + b
                    suf = (8 - a) * 128
                    rstar = 512 - a * 128
                    slot = gidx % 4
                    vsl = (2 * p16 + a // 4) % 3
                    voff = (a % 4) * 2048 + b * 256 + hh * 128
                    wl = [("ac", exp_ac[gidx]), ("qd", v_qd[(p16, a)])]
                    first = (a == 0 and b == 0)
                    lastc = (a == 7 and b == 7)
                    if a <= 3:
                        S.emit("tensor", lambda t, slot=slot, a=a, rs=rstar, first=first, b=b:
                               t.matmul(P[3][0:1, a * 128:512], ones_cb[:, :],
                                        PT[slot][:, 0:rs], start=first,
                                        stop=(a == 3 and b == 7)), waits=wl)
                        S.emit("tensor", lambda t, slot=slot, rs=rstar, suf=suf, first=first, lastc=lastc:
                               t.matmul(P[3][0:1, 512:1024], ones_cb[:, :],
                                        PT[slot][:, rs:suf], start=first, stop=lastc))
                        S.emit("tensor", lambda t, slot=slot, a=a, rs=rstar, vsl=vsl, voff=voff, first=first, b=b:
                               t.matmul(P[2][:, a * 128:512], VB[vsl][:, voff:voff + 128],
                                        PT[slot][:, 0:rs], start=first,
                                        stop=(a == 3 and b == 7)))
                        pv_pe[gidx] = S.emit(
                            "tensor", lambda t, slot=slot, rs=rstar, suf=suf, vsl=vsl, voff=voff, first=first, lastc=lastc:
                            t.matmul(P[2][:, 512:1024], VB[vsl][:, voff:voff + 128],
                                     PT[slot][:, rs:suf], start=first, stop=lastc),
                            inc="pe")
                    else:
                        S.emit("tensor", lambda t, slot=slot, a=a, suf=suf, lastc=lastc:
                               t.matmul(P[3][0:1, a * 128:1024], ones_cb[:, :],
                                        PT[slot][:, 0:suf], start=False, stop=lastc),
                               waits=wl)
                        pv_pe[gidx] = S.emit(
                            "tensor", lambda t, slot=slot, a=a, suf=suf, vsl=vsl, voff=voff, lastc=lastc:
                            t.matmul(P[2][:, a * 128:1024], VB[vsl][:, voff:voff + 128],
                                     PT[slot][:, 0:suf], start=False, stop=lastc),
                            inc="pe")

                chunks = [(a, b) for a in range(8) for b in range(8)]
                for ci, (a, b) in enumerate(chunks):
                    emit_qk(h, hh, p16, a, b)
                    if ci >= 1:
                        pa, pb = chunks[ci - 1]
                        emit_lpv(h, hh, p16, pa, pb)
                emit_lpv(h, hh, p16, 7, 7)
                for a in range(8):
                    g = 2 * p16 + (a // 4)
                    half_last_pv[g] = pv_pe[h * 64 + a * 8 + 7]
                head_last_pe[h] = pv_pe[h * 64 + 63]
                hl = head_last_pe[h]
                S.emit("vector", lambda v: v.tensor_scalar_mul(
                    r_tmp[:, 0:512], P[3][0:1, 0:512], L_SHIFT), waits=[("pe", hl)])
                S.emit("vector", lambda v: v.tensor_scalar_mul(
                    r_tmp[:, 512:1024], P[3][0:1, 512:1024], L_SHIFT))
                S.emit("vector", lambda v: v.reciprocal(r_rstd[:, :], r_tmp[:, :]))
                rl = S.emit("vector", lambda v: v.tensor_scalar_mul(
                    r_brow[:, :], r_rstd[:, :], L_SHIFT), inc="ve")
                S.emit("tensor", lambda t: t.matmul(
                    P[3][:, 0:512], ones_rf[:, :], r_brow[0:1, 0:512], start=True, stop=True),
                    waits=[("ve", rl)])
                rb = S.emit("tensor", lambda t: t.matmul(
                    P[3][:, 512:1024], ones_rf[:, :], r_brow[0:1, 512:1024], start=True, stop=True),
                    inc="pe")
                wrlb = [("pe", rb)]
                if h >= 1:
                    wrlb.append(("ve", ctx_ve[h - 1]))
                else:
                    wrlb.append(("ve", a_end["ve"]))
                rlb_ac[h] = S.emit("scalar", lambda sc: sc.activation(
                    RLB[:, :], P[3][:, :], AF.Identity), inc="ac", waits=wrlb)
                S.emit("vector", lambda v, h=h: v.tensor_mul(
                    CTX[:, h * 1024: h * 1024 + 512], P[2][:, 0:512], RLB[:, 0:512]),
                    waits=[("ac", rlb_ac[h])])
                ctx_ve[h] = S.emit("vector", lambda v, h=h: v.tensor_mul(
                    CTX[:, h * 1024 + 512: (h + 1) * 1024], P[2][:, 512:1024], RLB[:, 512:1024]),
                    inc="ve")
            pair_last = head_last_pe[2 * p16 + 1]
        ctx_all = ctx_ve[31]
        b_end_pe = head_last_pe[31]
        if debug:
            S.dma("sync", lambda sy: sy.dma_start(out=dbg["d_ctx"], in_=CTX[:, :]),
                  waits=[("ve", ctx_all)])

        cmm = {}
        x2_ve = {}
        x2w = {}
        csq = {}
        cstat = {}
        for of in range(32):
            par = of % 2
            wfree = [("pe", cmm[of - 2])] if of >= 2 else \
                    [("pe", b_end_pe), ("ve", ctx_all), ("ac", rlb_ac[31])]
            cwld = S.dma("gpsimd", lambda g, of=of, par=par: g.dma_start(
                out=WST[par][:, :], in_=out_wt[of, :, :]), waits=wfree)
            wx = [("ve", x2_ve[of - 2])] if of >= 2 else [("ve", ctx_all)]
            cxld = S.dma("sync", lambda sy, of=of, par=par: sy.dma_start(
                out=XLD[par][:, :], in_=x_fm[of * 128:(of + 1) * 128, :]), waits=wx)
            w0 = [("gd", cwld), ("ve", ctx_all)]
            if of >= 2:
                w0.append(("ve", x2_ve[of - 2]))
            for kc in range(32):
                st0, sp0 = (kc == 0), (kc == 31)
                S.emit("tensor", lambda t, par=par, kc=kc, st0=st0, sp0=sp0: t.matmul(
                    P[par][:, 0:512], WST[par][:, kc * 128:(kc + 1) * 128],
                    CTX[:, kc * 1024: kc * 1024 + 512], start=st0, stop=sp0),
                    waits=(w0 if kc == 0 else []))
                cmm[of] = S.emit("tensor", lambda t, par=par, kc=kc, st0=st0, sp0=sp0: t.matmul(
                    P[par][:, 512:1024], WST[par][:, kc * 128:(kc + 1) * 128],
                    CTX[:, kc * 1024 + 512: (kc + 1) * 1024], start=st0, stop=sp0),
                    inc="pe" if sp0 else None)
            wv = [("pe", cmm[of]), ("qd", cxld)]
            if of >= 2:
                wv += [("qd", x2w[of - 2]), ("pe", cstat[of - 2]), ("ac", csq[of - 2])]
            x2_ve[of] = S.emit("vector", lambda v, par=par, of=of: v.scalar_tensor_tensor(
                X2S[par][:, :], P[par][:, :], outb[:, of:of + 1], XLD[par][:, :],
                ALU.add, ALU.add), inc="ve", waits=wv)
            x2w[of] = S.dma("sync", lambda sy, par=par, of=of: sy.dma_start(
                out=x2_sp[of * 128:(of + 1) * 128, :], in_=X2S[par][:, :]),
                waits=[("ve", x2_ve[of])])
            wsq = [("ve", x2_ve[of])]
            if of >= 1:
                wsq.append(("pe", cstat[of - 1]))
            csq[of] = S.emit("scalar", lambda s, par=par: s.activation(
                SQF[:, :], X2S[par][:, :], AF.Square), inc="ac", waits=wsq)
            st0, sp0 = (of == 0), (of == 31)
            S.emit("tensor", lambda t, par=par, st0=st0, sp0=sp0: t.matmul(
                P[2][0:1, 0:512], ones_cf[:, :], X2S[par][:, 0:512], start=st0, stop=sp0),
                waits=[("ve", x2_ve[of]), ("ac", csq[of])])
            S.emit("tensor", lambda t, par=par, st0=st0, sp0=sp0: t.matmul(
                P[2][0:1, 512:1024], ones_cf[:, :], X2S[par][:, 512:1024], start=st0, stop=sp0))
            S.emit("tensor", lambda t, st0=st0, sp0=sp0: t.matmul(
                P[3][0:1, 0:512], ones_cf[:, :], SQF[:, 0:512], start=st0, stop=sp0))
            cstat[of] = S.emit("tensor", lambda t, st0=st0, sp0=sp0: t.matmul(
                P[3][0:1, 512:1024], ones_cf[:, :], SQF[:, 512:1024], start=st0, stop=sp0),
                inc="pe")
        if debug:
            S.dma("sync", lambda sy: sy.dma_start(out=dbg["d_x2"], in_=x2_sp),
                  waits=[("qd", x2w[31]), ("ve", x2_ve[31])])

        bcast2 = emit_ln_rows(cstat[31])

        ln2_ve = {}
        for kc in range(32):
            sl = kc % 2
            w = [("pe", bcast2), ("qd", x2w[kc])]
            if kc >= 2:
                w.append(("ve", ln2_ve[kc - 2]))
            else:
                w.append(("ve", x2_ve[31]))
            xq2 = S.dma("sync", lambda sy, kc=kc, sl=sl: sy.dma_start(
                out=XLD[sl][:, :], in_=x2_sp[kc * 128:(kc + 1) * 128, :]), waits=w)
            S.emit("vector", lambda v, sl=sl: v.tensor_mul(
                SQF[:, :], XLD[sl][:, :], P[2][:, :]),
                waits=[("qd", xq2), ("pe", bcast2), ("ac", csq[31])])
            ln2_ve[kc] = S.emit("vector", lambda v, kc=kc: v.tensor_sub(
                CTX[:, kc * 1024:(kc + 1) * 1024], SQF[:, :], P[3][:, :]), inc="ve")
        ln2_done = ln2_ve[31]
        if debug:
            S.dma("sync", lambda sy: sy.dma_start(out=dbg["d_ln2"], in_=CTX[:, :]),
                  waits=[("ve", ln2_done)])

        aat_gd = S.dma("gpsimd", lambda g: g.dma_start(
            out=AAT[:, :].rearrange("p (e n) -> p e n", e=2),
            in_=a_allt_t.rearrange("e p n -> p e n")),
            waits=[("pe", b_end_pe), ("ve", ln2_done)])
        for e2 in range(2):
            bfr_gd = S.dma("gpsimd", lambda g, e2=e2: g.dma_start(
                out=BFR[e2][:, :], in_=bfr_t[e2, :, :]))
        wv16 = S.emit("vector", lambda v: v.tensor_mul(
            w16_sb[:, :], cf_sb[:, :], sc_sb[:, :]), inc="ve",
            waits=[("qd", pre_qd), ("ve", ln2_done)])
        wer_ac = {}
        for e2 in range(2):
            ww = [("ve", wv16)]
            if e2 == 1:
                ww.append(("ac", wer_ac[0]))
            wm = S.emit("tensor", lambda t, e2=e2: t.matmul(
                P[3][:, e2:e2 + 1], oh_sb[:, e2 * 128:(e2 + 1) * 128], w16_sb[:, :],
                start=True, stop=True), inc="pe", waits=ww)
            wer_ac[e2] = S.emit("scalar", lambda s, e2=e2: s.activation(
                wer_sb[:, e2:e2 + 1], P[3][:, e2:e2 + 1], AF.Identity), inc="ac",
                waits=[("pe", wm)])
        xa_ac = {}
        for e2 in range(2):
            w0 = [("gd", aat_gd), ("ve", ln2_done)]
            if e2 == 1:
                w0.append(("ac", xa_ac[0]))
            xam = None
            for kc in range(32):
                st0, sp0 = (kc == 0), (kc == 31)
                S.emit("tensor", lambda t, e2=e2, kc=kc, st0=st0, sp0=sp0: t.matmul(
                    P[2][:, 0:512], AAT[:, (e2 * 32 + kc) * 128:(e2 * 32 + kc + 1) * 128],
                    CTX[:, kc * 1024: kc * 1024 + 512], start=st0, stop=sp0),
                    waits=(w0 if kc == 0 else []))
                xam = S.emit("tensor", lambda t, e2=e2, kc=kc, st0=st0, sp0=sp0: t.matmul(
                    P[2][:, 512:1024], AAT[:, (e2 * 32 + kc) * 128:(e2 * 32 + kc + 1) * 128],
                    CTX[:, kc * 1024 + 512: (kc + 1) * 1024], start=st0, stop=sp0),
                    inc="pe" if sp0 else None)
            xa_ac[e2] = S.emit("scalar", lambda s, e2=e2: s.activation(
                XA[e2][:, :], P[2][:, :], AF.Identity, scale=wer_sb[:, e2:e2 + 1]),
                inc="ac", waits=[("pe", xam), ("ac", wer_ac[e2])])

        dmm = {}
        dout_ve = {}
        outw = {}
        for of in range(32):
            par = of % 2
            wfree = [("pe", dmm[of - 2])] if of >= 2 else [("pe", cmm[30 + of])]
            dwld = S.dma("gpsimd", lambda g, of=of, par=par: g.dma_start(
                out=WST[par][:, :], in_=ffn_wtt[of, :, :]), waits=wfree)
            wx = [("qd", x2w[of])]
            if of >= 2:
                wx.append(("ve", dout_ve[of - 2]))
            else:
                wx.append(("ve", ln2_done))
            dxld = S.dma("sync", lambda sy, of=of, par=par: sy.dma_start(
                out=XLD[par][:, :], in_=x2_sp[of * 128:(of + 1) * 128, :]), waits=wx)
            w0 = [("gd", dwld), ("ac", xa_ac[1]), ("ve", ln2_done)]
            if of >= 2:
                w0.append(("ve", dout_ve[of - 2]))
            for kc in range(32):
                st0 = (kc == 0)
                S.emit("tensor", lambda t, par=par, kc=kc, st0=st0: t.matmul(
                    P[par][:, 0:512], WST[par][:, kc * 128:(kc + 1) * 128],
                    CTX[:, kc * 1024: kc * 1024 + 512], start=st0, stop=False),
                    waits=(w0 if kc == 0 else []))
                S.emit("tensor", lambda t, par=par, kc=kc, st0=st0: t.matmul(
                    P[par][:, 512:1024], WST[par][:, kc * 128:(kc + 1) * 128],
                    CTX[:, kc * 1024 + 512: (kc + 1) * 1024], start=st0, stop=False))
            for e2 in range(2):
                sp0 = (e2 == 1)
                S.emit("tensor", lambda t, par=par, e2=e2, of=of, sp0=sp0: t.matmul(
                    P[par][:, 0:512], BFR[e2][:, of * 128:(of + 1) * 128],
                    XA[e2][:, 0:512], start=False, stop=sp0))
                dmm[of] = S.emit("tensor", lambda t, par=par, e2=e2, of=of, sp0=sp0: t.matmul(
                    P[par][:, 512:1024], BFR[e2][:, of * 128:(of + 1) * 128],
                    XA[e2][:, 512:1024], start=False, stop=sp0),
                    inc="pe" if sp0 else None)
            wv = [("pe", dmm[of]), ("qd", dxld)]
            if of >= 2:
                wv.append(("qd", outw[of - 2]))
            dout_ve[of] = S.emit("vector", lambda v, par=par, of=of: v.scalar_tensor_tensor(
                X2S[par][:, :], P[par][:, :], ffnb[:, of:of + 1], XLD[par][:, :],
                ALU.add, ALU.add), inc="ve", waits=wv)
            outw[of] = S.dma("sync", lambda sy, par=par, of=of: sy.dma_start(
                out=out_fm[of * 128:(of + 1) * 128, :], in_=X2S[par][:, :]),
                waits=[("ve", dout_ve[of])])

        S.emit("sync", None, waits=[("qd", S.n["qd"])])

        block.tensor(lambda e: S.materialize(e, "tensor", sems))
        block.vector(lambda e: S.materialize(e, "vector", sems))
        block.scalar(lambda e: S.materialize(e, "scalar", sems))
        block.gpsimd(lambda e: S.materialize(e, "gpsimd", sems))
        block.sync(lambda e: S.materialize(e, "sync", sems))

    return nc



def _prep_inputs(inputs):
    x = np.asarray(inputs["L0_attn_hidden_states"], dtype=np.float32)[0]
    qkv_w = np.asarray(inputs["L0_attn_qkv_w"], dtype=np.float32)
    qkv_b = np.asarray(inputs["L0_attn_qkv_b"], dtype=np.float32)
    out_w = np.asarray(inputs["L0_attn_out_w"], dtype=np.float32)
    out_b = np.asarray(inputs["L0_attn_out_b"], dtype=np.float32)
    ffn_W = np.asarray(inputs["L0_ffn_W"], dtype=np.float32)
    ffn_b = np.asarray(inputs["L0_ffn_bias"], dtype=np.float32)
    A = np.asarray(inputs["L0_ffn_A"], dtype=np.float32)
    Bf = np.asarray(inputs["L0_ffn_Bf"], dtype=np.float32)
    cf = np.ascontiguousarray(np.asarray(inputs["L0_ffn_coeffs"], dtype=np.float32).reshape(16, 1))
    sc = np.ascontiguousarray(np.asarray(inputs["L0_ffn_scales"], dtype=np.float32).reshape(16, 1))

    C = np.ascontiguousarray
    qkv_wt = C(qkv_w.reshape(32, 128, 96, 128).transpose(2, 1, 0, 3).reshape(96, 128, 4096))
    out_wt = C(out_w.reshape(32, 128, 32, 128).transpose(2, 1, 0, 3).reshape(32, 128, 4096))
    ffn_wtt = C(ffn_W.T.reshape(32, 128, 32, 128).transpose(2, 1, 0, 3).reshape(32, 128, 4096))
    a_allt_t = C(A.reshape(256, 4096).T.reshape(32, 128, 2, 128)
                 .transpose(2, 1, 0, 3).reshape(2, 128, 4096))
    bfr_t = C(Bf.transpose(0, 2, 1).reshape(2, 128, 4096))
    qkvb_fm = C(qkv_b.reshape(96, 128).T)
    outb_fm = C(out_b.reshape(32, 128).T)
    ffnb_fm = C(ffn_b.reshape(32, 128).T)

    oh = np.zeros((16, 256), dtype=np.float32)
    for e2 in range(2):
        for p in range(128):
            oh[e2 * 8 + p // 16, e2 * 128 + p] = 1.0
    ident = np.eye(128, dtype=np.float32)
    swap_m = np.zeros((128, 128), dtype=np.float32)
    swap_m[np.arange(128), (np.arange(128) + 64) % 128] = 1.0
    inv = 10000.0 ** (-np.arange(64, dtype=np.float64) / 64.0)

    in_maps = []
    token_idx_all = []
    for c in range(NCORE):
        blocks = [8 * t + c for t in range(NT)]
        tok = np.concatenate([np.arange(b * 128, (b + 1) * 128) for b in blocks])
        token_idx_all.append(tok)
        ang = np.outer(inv, tok.astype(np.float64))
        cs = np.concatenate([np.cos(ang), np.cos(ang)], axis=0).astype(np.float32)
        csw = np.concatenate([np.sin(ang), -np.sin(ang)], axis=0).astype(np.float32)
        maskT = np.zeros((128, 1024), dtype=np.float32)
        for b in range(8):
            if b == c:
                maskT[:, b * 128:(b + 1) * 128] = np.where(
                    np.arange(128)[:, None] > np.arange(128)[None, :],
                    np.float32(MASKV), np.float32(0.0))
            elif b > c:
                maskT[:, b * 128:(b + 1) * 128] = MASKV
        in_maps.append({
            "x_fm": C(x[tok, :].T),
            "qkv_wt": qkv_wt, "qkvb_fm": qkvb_fm,
            "out_wt": out_wt, "outb_fm": outb_fm,
            "ffn_wtt": ffn_wtt, "ffnb_fm": ffnb_fm,
            "a_allt_t": a_allt_t, "bfr_t": bfr_t,
            "cf_in": cf, "sc_in": sc,
            "cs_in": cs, "csw_in": csw,
            "maskT_in": maskT, "id_in": ident, "oh_in": oh, "swap_in": swap_m,
        })
    return in_maps, token_idx_all


def kernel(**inputs):
    in_maps, token_idx_all = _prep_inputs(inputs)
    nc = build_program(debug=False)
    res = run_bass_kernel_spmd(nc, in_maps, core_ids=list(range(NCORE)))
    out = np.empty((SEQ, D), dtype=np.float32)
    for c in range(NCORE):
        out[token_idx_all[c], :] = res.results[c]["out_fm"].T
    return out.reshape(1, SEQ, D)


# version 07
# speedup vs baseline: 1.0771x; 1.0339x over previous
import math
from contextlib import ExitStack

import numpy as np

import concourse.bass as bass
import concourse.mybir as mybir
from concourse.bass_utils import run_bass_kernel_spmd

F32 = mybir.dt.float32
BF16 = mybir.dt.bfloat16
AF = mybir.ActivationFunctionType
ALU = mybir.AluOpType

D = 4096
SEQ = 8192
NCORE = 8
TPC = 1024
NT = 8
SCALE = 1.0 / math.sqrt(128.0)
MASKV = -66504.0
EPS = 1e-5
INV_D = 1.0 / 4096.0
L_SHIFT = float(2.0 ** 24)

ENGS = ("tensor", "vector", "scalar", "gpsimd", "sync")
OWN = {"tensor": "pe", "vector": "ve", "scalar": "ac", "gpsimd": "po"}


class Sched:

    def __init__(self):
        self.streams = {e: [] for e in ENGS}
        self.n = {s: 0 for s in ("pe", "ve", "ac", "po", "qd", "gd",
                                 "ck", "cv", "cn")}

    def emit(self, eng, fn, inc=None, inc_by=1, waits=()):
        ws = [(s, v) for (s, v) in waits if v and v > 0 and s != OWN.get(eng)]
        self.streams[eng].append((fn, inc, inc_by, ws))
        if inc:
            self.n[inc] += inc_by
            return self.n[inc]
        return None

    def dma(self, eng, fn, waits=()):
        q = "qd" if eng == "sync" else "gd"
        return self.emit(eng, fn, inc=q, inc_by=16, waits=waits)

    def snap(self):
        return dict(self.n)

    def materialize(self, eng_obj, eng_name, sems):
        seen = {}
        for fn, inc, inc_by, ws in self.streams[eng_name]:
            for s, v in ws:
                if seen.get(s, 0) < v:
                    eng_obj.wait_ge(sems[s], v)
                    seen[s] = v
            ins = fn(eng_obj) if fn is not None else None
            if inc:
                if ins is None:
                    ins = eng_obj.nop()
                ins.then_inc(sems[inc], inc_by)


def build_program(debug=False):
    nc = bass.Bass("TRN2", target_bir_lowering=False, debug=False)

    def din(name, shape):
        return nc.dram_tensor(name, shape, F32, kind="ExternalInput").ap()

    x_fm = din("x_fm", [D, TPC])
    qkv_wt = din("qkv_wt", [96, 128, 4096])
    qkvb_fm = din("qkvb_fm", [128, 96])
    out_wt = din("out_wt", [32, 128, 4096])
    outb_fm = din("outb_fm", [128, 32])
    ffn_wtt = din("ffn_wtt", [32, 128, 4096])
    ffnb_fm = din("ffnb_fm", [128, 32])
    a_allt_t = din("a_allt_t", [2, 128, 4096])
    bfr_t = din("bfr_t", [2, 128, 4096])
    cf_in = din("cf_in", [16, 1])
    sc_in = din("sc_in", [16, 1])
    cs_in = din("cs_in", [128, TPC])
    csw_in = din("csw_in", [128, TPC])
    swap_in = din("swap_in", [128, 128])
    maskT_in = din("maskT_in", [128, 1024])
    id_in = din("id_in", [128, 128])
    oh_in = din("oh_in", [16, 256])

    out_fm = nc.dram_tensor("out_fm", [D, TPC], F32, kind="ExternalOutput").ap()

    def dint(name, shape, dt=BF16):
        return nc.dram_tensor(name, shape, dt, kind="Internal").ap()

    q_spill = dint("q_spill", [D, TPC])
    k_in = dint("k_in", [D, TPC])
    v_in = dint("v_in", [TPC, D])
    k_all = dint("k_all", [NCORE * D, TPC])
    v_all = dint("v_all", [SEQ, D])
    kn_i = dint("kn_i", [1, 32], F32)
    knm_i = dint("knm_i", [1, 32], F32)
    x2_sp = dint("x2_sp", [D, TPC], F32)

    dbg = {}
    if debug:
        for nm, shape, dt in [
            ("d_ln1", [128, 32 * 1024], BF16), ("d_ctx", [128, 32 * 1024], BF16),
            ("d_kall", [NCORE * D, TPC], BF16), ("d_vall", [SEQ, D], BF16),
            ("d_qsp", [D, TPC], BF16), ("d_bias", [128, 32], F32),
            ("d_x2", [D, TPC], F32), ("d_ln2", [128, 32 * 1024], BF16),
        ]:
            dbg[nm] = nc.dram_tensor(nm, shape, dt, kind="ExternalOutput").ap()

    with ExitStack() as stk:
        E = stk.enter_context
        AB = E(nc.sbuf_tensor("AB", [128, 49152], BF16))
        CTX = E(nc.sbuf_tensor("CTX", [128, 32768], BF16))
        AFR = E(nc.sbuf_tensor("AFR", [128, 3072], F32))
        cs_sb = E(nc.sbuf_tensor("cs_sb", [128, TPC], F32))
        csw_sb = E(nc.sbuf_tensor("csw_sb", [128, TPC], F32))
        maskT = E(nc.sbuf_tensor("maskT", [128, 1024], BF16))
        id_sb = E(nc.sbuf_tensor("id_sb", [128, 128], BF16))
        swp_sb = E(nc.sbuf_tensor("swp_sb", [128, 128], BF16))
        qkvb = E(nc.sbuf_tensor("qkvb", [128, 96], F32))
        outb = E(nc.sbuf_tensor("outb", [128, 32], F32))
        ffnb = E(nc.sbuf_tensor("ffnb", [128, 32], F32))
        bias_sb = E(nc.sbuf_tensor("bias_sb", [128, 32], F32))
        wer_sb = E(nc.sbuf_tensor("wer_sb", [128, 2], F32))
        oh_sb = E(nc.sbuf_tensor("oh_sb", [16, 256], F32))
        cf_sb = E(nc.sbuf_tensor("cf_sb", [16, 1], F32))
        sc_sb = E(nc.sbuf_tensor("sc_sb", [16, 1], F32))
        w16_sb = E(nc.sbuf_tensor("w16_sb", [16, 1], F32))
        ones_cf = E(nc.sbuf_tensor("ones_cf", [128, 1], F32))
        ones_cb = E(nc.sbuf_tensor("ones_cb", [128, 1], BF16))
        ones_rf = E(nc.sbuf_tensor("ones_rf", [1, 128], F32))
        ROWS = E(nc.sbuf_tensor("ROWS", [1, 4096], F32))
        ROW2 = E(nc.sbuf_tensor("ROW2", [1, 256], F32))
        CST = E(nc.sbuf_tensor("CST", [1, 8], F32))

        P = [E(nc.psum_tensor(f"P{i}", [128, 1024], F32)) for i in range(4)]
        P2bf = P[2].ap().bitcast(BF16)

        sems = {}
        for s in ("pe", "ve", "ac", "po", "qd", "gd", "ck", "cv", "cn"):
            sems[s] = E(nc.semaphore(s + "_s"))
        block = E(nc.Block())

        S = Sched()

        r_mean = ROWS[0:1, 0:1024]
        r_tmp = ROWS[0:1, 1024:2048]
        r_rstd = ROWS[0:1, 2048:3072]
        r_brow = ROWS[0:1, 3072:4096]
        r_kn = ROW2[0:1, 0:32]
        r_knm = ROW2[0:1, 32:64]
        r_qn = ROW2[0:1, 64:96]
        r_b1 = ROW2[0:1, 96:128]
        r_b2 = ROW2[0:1, 128:160]
        r_b3 = ROW2[0:1, 160:192]

        def ab(c0, w):
            return AB[:, c0:c0 + w]

        def afr(c0, w):
            return AFR[:, c0:c0 + w]

        LN1 = lambda kc: ab(kc * 1024, 1024)
        WST = [ab(32768, 4096), ab(36864, 4096)]
        ROPE = [ab(40960, 1024), ab(41984, 1024)]
        VSTG = [ab(43008, 1024), ab(44032, 1024)]
        VTM = [ab(45056, 1024), ab(46080, 1024)]
        SQK = ab(47104, 1024)
        KB = [ab(0, 8192), ab(8192, 8192)]
        VB = [ab(16384, 8192), ab(24576, 8192), ab(32768, 8192)]
        QB = [ab(40960, 1024), ab(41984, 1024)]
        PT = [ab(43008 + i * 1024, 1024) for i in range(4)]
        XA = [ab(0, 1024), ab(1024, 1024)]
        BFR = [ab(2048, 4096), ab(6144, 4096)]
        AAT = ab(10240, 8192)

        XLD = [afr(0, 1024), afr(1024, 1024)]
        TA = XLD[0]
        TB = XLD[1]
        SQF = afr(2048, 1024)
        RLB = afr(2048, 1024)
        X2S = XLD

        for dst, src in ((cs_sb, cs_in), (csw_sb, csw_in), (qkvb, qkvb_fm),
                         (outb, outb_fm), (ffnb, ffnb_fm), (oh_sb, oh_in),
                         (cf_sb, cf_in), (sc_sb, sc_in)):
            S.dma("sync", lambda sy, d=dst, s=src: sy.dma_start(out=d[:, :], in_=s))
        pre_qd = S.n["qd"]
        for dst, src in ((maskT, maskT_in), (id_sb, id_in), (swp_sb, swap_in)):
            S.dma("gpsimd", lambda g, d=dst, s=src: g.dma_start(out=d[:, :], in_=s))
        S.emit("gpsimd", lambda g: g.memset(CST[0:1, 0:1], EPS))
        S.emit("gpsimd", lambda g: g.memset(CST[0:1, 1:2], SCALE))
        nc.const_aps.register(F32, EPS, CST[0:1, 0:1])
        nc.const_aps.register(F32, SCALE, CST[0:1, 1:2])
        S.emit("gpsimd", lambda g: g.memset(ones_cf[:, :], 1.0))
        S.emit("gpsimd", lambda g: g.memset(ones_cb[:, :], 1.0))
        pre_po = S.emit("gpsimd", lambda g: g.memset(ones_rf[:, :], 1.0), inc="po")

        xq = {}
        stat_pe = {}
        sq_ac = {}
        for c in range(32):
            sl = c % 2
            w = [("pe", stat_pe[c - 2]), ("ac", sq_ac[c - 2])] if c >= 2 else []
            xq[c] = S.dma("sync", lambda sy, c=c, sl=sl: sy.dma_start(
                out=XLD[sl][:, :], in_=x_fm[c * 128:(c + 1) * 128, :]), waits=w)
            wsq = [("qd", xq[c])]
            if c >= 1:
                wsq.append(("pe", stat_pe[c - 1]))
            if c == 0:
                wsq.append(("po", pre_po))
            sq_ac[c] = S.emit("scalar", lambda s, sl=sl: s.activation(
                SQF[:, :], XLD[sl][:, :], AF.Square), inc="ac", waits=wsq)
            st0, sp0 = (c == 0), (c == 31)
            S.emit("tensor", lambda t, sl=sl, st0=st0, sp0=sp0: t.matmul(
                P[2][0:1, 0:512], ones_cf[:, :], XLD[sl][:, 0:512], start=st0, stop=sp0),
                waits=[("qd", xq[c]), ("po", pre_po)])
            S.emit("tensor", lambda t, sl=sl, st0=st0, sp0=sp0: t.matmul(
                P[2][0:1, 512:1024], ones_cf[:, :], XLD[sl][:, 512:1024], start=st0, stop=sp0))
            S.emit("tensor", lambda t, st0=st0, sp0=sp0: t.matmul(
                P[3][0:1, 0:512], ones_cf[:, :], SQF[:, 0:512], start=st0, stop=sp0),
                waits=[("ac", sq_ac[c])])
            stat_pe[c] = S.emit("tensor", lambda t, st0=st0, sp0=sp0: t.matmul(
                P[3][0:1, 512:1024], ones_cf[:, :], SQF[:, 512:1024], start=st0, stop=sp0),
                inc="pe")

        def emit_ln_rows(stat_wait):
            S.emit("vector", lambda v: v.tensor_scalar_mul(
                r_mean[:, :], P[2][0:1, 0:1024], INV_D), waits=[("pe", stat_wait)])
            S.emit("vector", lambda v: v.tensor_mul(r_tmp[:, :], r_mean[:, :], r_mean[:, :]))
            vv = S.emit("vector", lambda v: v.scalar_tensor_tensor(
                r_brow[:, :], P[3][0:1, 0:1024], INV_D, r_tmp[:, :], ALU.mult, ALU.subtract),
                inc="ve")
            aa = S.emit("scalar", lambda s: s.activation(
                r_tmp[:, :], r_brow[:, :], AF.Sqrt, bias=EPS), inc="ac", waits=[("ve", vv)])
            S.emit("vector", lambda v: v.reciprocal(r_rstd[:, :], r_tmp[:, :]),
                   waits=[("ac", aa)])
            rw = S.emit("vector", lambda v: v.tensor_mul(
                r_brow[:, :], r_mean[:, :], r_rstd[:, :]), inc="ve")
            S.emit("tensor", lambda t: t.matmul(
                P[2][:, 0:512], ones_rf[:, :], r_rstd[:, 0:512], start=True, stop=True),
                waits=[("ve", rw), ("po", pre_po)])
            S.emit("tensor", lambda t: t.matmul(
                P[2][:, 512:1024], ones_rf[:, :], r_rstd[:, 512:1024], start=True, stop=True))
            S.emit("tensor", lambda t: t.matmul(
                P[3][:, 0:512], ones_rf[:, :], r_brow[:, 0:512], start=True, stop=True))
            return S.emit("tensor", lambda t: t.matmul(
                P[3][:, 512:1024], ones_rf[:, :], r_brow[:, 512:1024], start=True, stop=True),
                inc="pe")

        bcast_pe = emit_ln_rows(stat_pe[31])

        ln1_ve = {}
        for c in range(32):
            sl = c % 2
            w = [("pe", bcast_pe)]
            if c >= 2:
                w.append(("ve", ln1_ve[c - 2]))
            xq2 = S.dma("sync", lambda sy, c=c, sl=sl: sy.dma_start(
                out=XLD[sl][:, :], in_=x_fm[c * 128:(c + 1) * 128, :]), waits=w)
            S.emit("vector", lambda v, sl=sl: v.tensor_mul(
                SQF[:, :], XLD[sl][:, :], P[2][:, :]),
                waits=[("qd", xq2), ("pe", bcast_pe)])
            ln1_ve[c] = S.emit("vector", lambda v, c=c: v.tensor_sub(
                LN1(c)[:, :], SQF[:, :], P[3][:, :]), inc="ve")
        ln1_done = ln1_ve[31]
        if debug:
            S.dma("sync", lambda sy: sy.dma_start(out=dbg["d_ln1"], in_=AB[:, 0:32768]),
                  waits=[("ve", ln1_done)])

        qkv_mm = {}
        chunk_free = {}
        last_rope_dma = {}
        last_sqk = {"mm": 0, "rd": ln1_done}
        last_rope_mm = {"pe": 0, "ve": 0, "evac": 0}
        wld = {}

        CTXF = CTX.ap().bitcast(F32)
        WSTG = [CTXF[:, i * 4096:(i + 1) * 4096] for i in range(4)]
        wcast = {}

        def emit_wload(idx, c):
            wfree = [("ac", wcast[idx - 4])] if idx >= 4 else []
            wld[idx] = S.dma("sync", lambda sy, c=c, sl=idx % 4: sy.dma_start(
                out=WSTG[sl][:, :], in_=qkv_wt[c, :, :]), waits=wfree)

        def emit_wcast(idx):
            wc = [("qd", wld[idx])]
            if idx >= 2:
                wc.append(("pe", qkv_mm[idx - 2]))
            wcast[idx] = S.emit("scalar", lambda sc, sl4=idx % 4, sl=idx % 2:
                                sc.activation(WST[sl][:, :], WSTG[sl4][:, :],
                                              AF.Identity), inc="ac", waits=wc)

        def emit_wmms(idx):
            psl = idx % 2
            w0 = [("ac", wcast[idx]), ("ve", ln1_done)]
            if idx >= 2:
                w0.append(chunk_free[idx - 2])
            last = None
            for kc in range(32):
                st0, sp0 = (kc == 0), (kc == 31)
                S.emit("tensor", lambda t, sl=psl, kc=kc, st0=st0, sp0=sp0:
                       t.matmul(P[sl][:, 0:512], WST[sl][:, kc * 128:(kc + 1) * 128],
                                LN1(kc)[:, 0:512], start=st0, stop=sp0),
                       waits=(w0 if kc == 0 else []))
                last = S.emit("tensor", lambda t, sl=psl, kc=kc, st0=st0, sp0=sp0:
                              t.matmul(P[sl][:, 512:1024], WST[sl][:, kc * 128:(kc + 1) * 128],
                                       LN1(kc)[:, 512:1024], start=st0, stop=sp0),
                              inc="pe" if sp0 else None)
            qkv_mm[idx] = last

        def emit_rope_a(idx, c):
            psl = idx % 2
            w = [("pe", qkv_mm[idx]), ("pe", last_rope_mm["pe"])]
            S.emit("vector", lambda v, psl=psl, c=c: v.scalar_tensor_tensor(
                VSTG[0][:, :], P[psl][:, :], qkvb[:, c:c + 1], cs_sb[:, :],
                ALU.add, ALU.mult), waits=w)
            rv = S.emit("vector", lambda v, psl=psl, c=c: v.scalar_tensor_tensor(
                VSTG[1][:, :], P[psl][:, :], qkvb[:, c:c + 1], csw_sb[:, :],
                ALU.add, ALU.mult), inc="ve")
            chunk_free[idx] = ("ve", rv)
            last_rope_mm["ve"] = rv

        def emit_rope_b(idx, h, dest_dram, nrow):
            sl = idx % 2
            wmm = [("ve", last_rope_mm["ve"]), ("ac", last_rope_mm["evac"])]
            S.emit("tensor", lambda t: t.matmul(
                P[3][:, 0:512], id_sb[:, :], VSTG[0][:, 0:512], start=True, stop=False),
                waits=wmm)
            S.emit("tensor", lambda t: t.matmul(
                P[3][:, 0:512], swp_sb[:, :], VSTG[1][:, 0:512], start=False, stop=True))
            S.emit("tensor", lambda t: t.matmul(
                P[3][:, 512:1024], id_sb[:, :], VSTG[0][:, 512:1024], start=True, stop=False))
            rmm = S.emit("tensor", lambda t: t.matmul(
                P[3][:, 512:1024], swp_sb[:, :], VSTG[1][:, 512:1024], start=False, stop=True),
                inc="pe")
            last_rope_mm["pe"] = rmm
            wev = [("pe", rmm)]
            if sl in last_rope_dma:
                wev.append(("qd", last_rope_dma[sl]))
            ev = S.emit("scalar", lambda s, sl=sl: s.activation(
                ROPE[sl][:, :], P[3][:, :], AF.Identity), inc="ac", waits=wev)
            last_rope_mm["evac"] = ev
            last_rope_dma[sl] = S.dma("sync", lambda sy, sl=sl, h=h, dd=dest_dram:
                                      sy.dma_start(out=dd[h * 128:(h + 1) * 128, :],
                                                   in_=ROPE[sl][:, :]),
                                      waits=[("ac", ev)])
            wsq = [("ac", ev), ("pe", last_sqk["mm"])]
            sq = S.emit("scalar", lambda s, sl=sl: s.activation(
                SQK[:, :], ROPE[sl][:, :], AF.Square), inc="ac", waits=wsq)
            last_rope_dma[(sl, "sq")] = sq
            wmm2 = [("ac", sq), ("ve", last_sqk["rd"])]
            S.emit("tensor", lambda t: t.matmul(
                P[2][0:1, 0:512], ones_cb[:, :], SQK[:, 0:512], start=True, stop=True),
                waits=wmm2)
            last_sqk["mm"] = S.emit("tensor", lambda t: t.matmul(
                P[2][0:1, 512:1024], ones_cb[:, :], SQK[:, 512:1024], start=True, stop=True),
                inc="pe")
            last_sqk["rd"] = S.emit("vector", lambda v, h=h, nrow=nrow: v.reduce_max(
                nrow[:, h:h + 1], P[2][0:1, 0:1024], axis=mybir.AxisListType.X),
                inc="ve", waits=[("pe", last_sqk["mm"])])

        qkvcol = lambda idx: (32 + idx) if idx < 32 else ((64 + idx - 32) if idx < 64 else idx - 64)
        emit_wload(0, qkvcol(0))
        emit_wload(1, qkvcol(1))
        emit_wcast(0)
        for i in range(32):
            if i + 2 < 96:
                emit_wload(i + 2, qkvcol(i + 2))
            emit_wmms(i)
            if i >= 1:
                emit_rope_b(i - 1, i - 1, k_in, r_kn)
            emit_rope_a(i, 32 + i)
            emit_wcast(i + 1)
        emit_rope_b(31, 31, k_in, r_kn)
        k_done_qd = last_rope_dma[1]
        kn_qd = S.dma("sync", lambda sy: sy.dma_start(out=kn_i, in_=r_kn),
                      waits=[("ve", last_sqk["rd"])])
        k_nrm_rd = last_sqk["rd"]

        vtm_dma = {}
        vtm_ac = {}

        def emit_v_evac(i):
            idx = 32 + i
            h = i
            psl = idx % 2
            ve_ac = S.emit("scalar", lambda s, psl=psl, c=64 + i: s.activation(
                VSTG[psl][:, :], P[psl][:, :], AF.Identity, bias=qkvb[:, c:c + 1]),
                inc="ac", waits=[("pe", qkv_mm[idx])])
            chunk_free[idx] = ("ac", ve_ac)
            wtr = [("ac", ve_ac)]
            if i == 0:
                wtr.append(("ve", k_nrm_rd))
            if i >= 1:
                wtr.append(("ac", vtm_ac[i - 1]))
            last = None
            for j in range(8):
                last = S.emit("tensor", lambda t, psl=psl, j=j: t.transpose(
                    P2bf[:, j * 128:(j + 1) * 128], VSTG[psl][:, j * 128:(j + 1) * 128],
                    id_sb[:, :]), inc="pe" if j == 7 else None,
                    waits=(wtr if j == 0 else []))
            wvt = [("pe", last)]
            if i >= 2:
                wvt.append(("qd", vtm_dma[i - 2]))
            vtm_ac[i] = S.emit("scalar", lambda s, psl=psl: s.activation(
                VTM[psl][:, :], P2bf[:, 0:1024], AF.Identity), inc="ac", waits=wvt)
            vtm_dma[i] = S.dma("sync", lambda sy, psl=psl, h=h: sy.dma_start(
                out=v_in.rearrange("(j p) o -> p j o", p=128)[:, :, h * 128:(h + 1) * 128],
                in_=VTM[psl][:, :].rearrange("p (j o) -> p j o", j=8)),
                waits=[("ac", vtm_ac[i])])

        for i in range(32):
            idx = 32 + i
            if idx + 2 < 96:
                emit_wload(idx + 2, qkvcol(idx + 2))
            emit_wmms(idx)
            if i >= 1:
                emit_v_evac(i - 1)
            emit_wcast(idx + 1)
        emit_v_evac(31)
        v_done_qd = vtm_dma[31]

        for i in range(32):
            idx = 64 + i
            if idx + 2 < 96:
                emit_wload(idx + 2, qkvcol(idx + 2))
            emit_wmms(idx)
            if i >= 1:
                emit_rope_b(idx - 1, i - 1, q_spill, r_qn)
            emit_rope_a(idx, i)
            if idx + 1 < 96:
                emit_wcast(idx + 1)
        emit_rope_b(95, 31, q_spill, r_qn)
        q_nrm_rd = last_sqk["rd"]

        S.emit("gpsimd", lambda g: g.collective_compute(
            "AllGather", ALU.bypass, replica_groups=[list(range(NCORE))],
            ins=[k_in.tensor.ap().opt()], outs=[k_all.tensor.ap().opt()]),
            inc="ck", waits=[("qd", k_done_qd), ("pe", S.n["pe"]),
                             ("ve", S.n["ve"]), ("ac", S.n["ac"]),
                             ("qd", S.n["qd"])])
        S.emit("gpsimd", lambda g: g.collective_compute(
            "AllGather", ALU.bypass, replica_groups=[list(range(NCORE))],
            ins=[v_in.tensor.ap().opt()], outs=[v_all.tensor.ap().opt()]),
            inc="cv")
        S.emit("gpsimd", lambda g: g.collective_compute(
            "AllReduce", ALU.max, replica_groups=[list(range(NCORE))],
            ins=[kn_i.tensor.ap().opt()], outs=[knm_i.tensor.ap().opt()]),
            inc="cn")





        knm_qd = S.dma("sync", lambda sy: sy.dma_start(
            out=r_knm, in_=knm_i), waits=[("cn", 1)])
        bb = S.emit("vector", lambda v: v.tensor_mul(
            r_b1[:, :], r_qn[:, :], r_knm[:, :]),
            inc="ve", waits=[("qd", knm_qd), ("ve", q_nrm_rd)])
        bs = S.emit("scalar", lambda s: s.activation(
            r_b2[:, :], r_b1[:, :], AF.Sqrt), inc="ac", waits=[("ve", bb)])
        bv = S.emit("vector", lambda v: v.tensor_scalar_mul(
            r_b3[:, :], r_b2[:, :], -SCALE), inc="ve", waits=[("ac", bs)])
        bm = S.emit("tensor", lambda t: t.matmul(
            P[2][:, 0:32], ones_rf[:, :], r_b3[:, :], start=True, stop=True),
            inc="pe", waits=[("ve", bv), ("ve", q_nrm_rd)])
        bias_ac = S.emit("scalar", lambda s: s.activation(
            bias_sb[:, :], P[2][:, 0:32], AF.Identity), inc="ac", waits=[("pe", bm)])
        if debug:
            S.dma("sync", lambda sy: sy.dma_start(out=dbg["d_bias"], in_=bias_sb[:, :]),
                  waits=[("ac", bias_ac)])
            S.dma("sync", lambda sy: sy.dma_start(out=dbg["d_qsp"], in_=q_spill),
                  waits=[("qd", last_rope_dma[1])])
            S.dma("sync", lambda sy: sy.dma_start(out=dbg["d_kall"], in_=k_all),
                  waits=[("ck", 1)])
            S.dma("sync", lambda sy: sy.dma_start(out=dbg["d_vall"], in_=v_all),
                  waits=[("cv", 1)])

        a_end = S.snap()

        kq_qd = {}
        v_qd = {}
        half_last_pv = {}
        exp_ac = {}
        pv_pe = {}
        pending_tail = [None]
        ctx_ve = {}
        rlb_ac = {}
        head_last_pe = {}

        for p16 in range(16):
            for hh in range(2):
                h = 2 * p16 + hh
                wk = [("ck", 1), ("cv", 1), ("pe", a_end["pe"]),
                      ("ve", a_end["ve"]), ("ac", a_end["ac"])] if h < 2 else \
                     [("pe", head_last_pe[h - 2])]
                for b in range(8):
                    S.dma("sync", lambda sy, h=h, b=b: sy.dma_start(
                        out=KB[h % 2][:, b * 1024:(b + 1) * 1024],
                        in_=k_all[b * D + h * 128: b * D + (h + 1) * 128, :]),
                        waits=(wk if b == 0 else []))
                kq_qd[h] = S.dma("sync", lambda sy, h=h: sy.dma_start(
                    out=QB[h % 2][:, :], in_=q_spill[h * 128:(h + 1) * 128, :]))
            for a in range(8):
                g = 2 * p16 + (a // 4)
                vw = [("cv", 1)]
                if g < 2:
                    vw += [("pe", a_end["pe"]), ("ve", a_end["ve"]), ("ac", a_end["ac"])]
                else:
                    vw.append(("pe", half_last_pv.get(g - 3, 0)))
                v_qd[(p16, a)] = S.dma("sync", lambda sy, p16=p16, a=a, sl=(2 * p16 + a // 4) % 3:
                                       sy.dma_start(
                    out=VB[sl][:, (a % 4) * 2048:(a % 4 + 1) * 2048]
                        .rearrange("p (b c) -> p b c", b=8),
                    in_=v_all.rearrange("(b a p) o -> b a p o", a=8, p=128)
                        [:, a, :, p16 * 256:(p16 + 1) * 256].rearrange("b p o -> p b o")),
                    waits=vw)

            for hh in range(2):
                h = 2 * p16 + hh
                qb = QB[h % 2]
                kb = KB[h % 2]

                def emit_qk(h, hh, p16, a, b, qb=qb, kb=kb):
                    gidx = h * 64 + a * 8 + b
                    par = gidx % 2
                    suf = (8 - a) * 128
                    rstar = 512 - a * 128
                    s0 = P[0][:, par * 512:(par + 1) * 512]
                    s1 = P[1][:, par * 512:(par + 1) * 512]
                    kt = b * 1024 + a * 128
                    slot = gidx % 4
                    wqk = [("qd", kq_qd[h])]
                    if gidx >= 2:
                        wqk.append(("ac", exp_ac[gidx - 2]))
                    else:
                        wqk += [("ve", a_end["ve"]), ("ac", a_end["ac"])]
                    if a <= 3:
                        S.emit("tensor", lambda t, s0=s0, kt=kt, a=a, rs=rstar:
                               t.matmul(s0[:, 0:rs], kb[:, kt:kt + 128],
                                        qb[:, a * 128:512], start=True, stop=False),
                               waits=wqk)
                        S.emit("tensor", lambda t, s0=s0, b=b: t.matmul(
                            s0[:, 0:128], id_sb[:, :], maskT[:, b * 128:(b + 1) * 128],
                            start=False, stop=True))
                        qk_pe = S.emit("tensor", lambda t, s1=s1, kt=kt:
                                       t.matmul(s1[:, 0:512], kb[:, kt:kt + 128],
                                                qb[:, 512:1024], start=True, stop=True),
                                       inc="pe")
                    else:
                        S.emit("tensor", lambda t, s0=s0, kt=kt, a=a, suf=suf:
                               t.matmul(s0[:, 0:suf], kb[:, kt:kt + 128],
                                        qb[:, a * 128:1024], start=True, stop=False),
                               waits=wqk)
                        qk_pe = S.emit("tensor", lambda t, s0=s0, b=b: t.matmul(
                            s0[:, 0:128], id_sb[:, :], maskT[:, b * 128:(b + 1) * 128],
                            start=False, stop=True), inc="pe")
                    we = [("pe", qk_pe)]
                    if gidx >= 4:
                        we.append(("pe", pv_pe.get(gidx - 4, 0)))
                    else:
                        we.append(("qd", v_done_qd))
                    if a <= 3:
                        S.emit("scalar", lambda sc, s0=s0, slot=slot, h=h, rs=rstar:
                               sc.activation(PT[slot][:, 0:rs], s0[:, 0:rs], AF.Exp,
                                             bias=bias_sb[:, h:h + 1], scale=SCALE),
                               waits=we)
                        exp_ac[gidx] = S.emit(
                            "scalar", lambda sc, s1=s1, slot=slot, h=h, rs=rstar, suf=suf:
                            sc.activation(PT[slot][:, rs:suf], s1[:, 0:512], AF.Exp,
                                          bias=bias_sb[:, h:h + 1], scale=SCALE),
                            inc="ac")
                    else:
                        exp_ac[gidx] = S.emit(
                            "scalar", lambda sc, s0=s0, slot=slot, h=h, suf=suf:
                            sc.activation(PT[slot][:, 0:suf], s0[:, 0:suf], AF.Exp,
                                          bias=bias_sb[:, h:h + 1], scale=SCALE),
                            inc="ac", waits=we)

                def emit_lpv(h, hh, p16, a, b):
                    gidx = h * 64 + a * 8 + b
                    suf = (8 - a) * 128
                    rstar = 512 - a * 128
                    slot = gidx % 4
                    vsl = (2 * p16 + a // 4) % 3
                    voff = (a % 4) * 2048 + b * 256 + hh * 128
                    wl = [("ac", exp_ac[gidx]), ("qd", v_qd[(p16, a)])]
                    first = (a == 0 and b == 0)
                    if first and h > 0:
                        wl += [("ve", ctx_ve[h - 1]), ("ac", rlb_ac[h - 1])]
                    lastc = (a == 7 and b == 7)
                    if a <= 3:
                        S.emit("tensor", lambda t, slot=slot, a=a, rs=rstar, first=first, b=b:
                               t.matmul(P[3][0:1, a * 128:512], ones_cb[:, :],
                                        PT[slot][:, 0:rs], start=first,
                                        stop=(a == 3 and b == 7)), waits=wl)
                        S.emit("tensor", lambda t, slot=slot, rs=rstar, suf=suf, first=first, lastc=lastc:
                               t.matmul(P[3][0:1, 512:1024], ones_cb[:, :],
                                        PT[slot][:, rs:suf], start=first, stop=lastc))
                        S.emit("tensor", lambda t, slot=slot, a=a, rs=rstar, vsl=vsl, voff=voff, first=first, b=b:
                               t.matmul(P[2][:, a * 128:512], VB[vsl][:, voff:voff + 128],
                                        PT[slot][:, 0:rs], start=first,
                                        stop=(a == 3 and b == 7)))
                        pv_pe[gidx] = S.emit(
                            "tensor", lambda t, slot=slot, rs=rstar, suf=suf, vsl=vsl, voff=voff, first=first, lastc=lastc:
                            t.matmul(P[2][:, 512:1024], VB[vsl][:, voff:voff + 128],
                                     PT[slot][:, rs:suf], start=first, stop=lastc),
                            inc="pe")
                    else:
                        S.emit("tensor", lambda t, slot=slot, a=a, suf=suf, lastc=lastc:
                               t.matmul(P[3][0:1, a * 128:1024], ones_cb[:, :],
                                        PT[slot][:, 0:suf], start=False, stop=lastc),
                               waits=wl)
                        pv_pe[gidx] = S.emit(
                            "tensor", lambda t, slot=slot, a=a, suf=suf, vsl=vsl, voff=voff, lastc=lastc:
                            t.matmul(P[2][:, a * 128:1024], VB[vsl][:, voff:voff + 128],
                                     PT[slot][:, 0:suf], start=False, stop=lastc),
                            inc="pe")

                chunks = [(a, b) for a in range(8) for b in range(8)]
                for ci, (a, b) in enumerate(chunks):
                    emit_qk(h, hh, p16, a, b)
                    if ci == 2 and pending_tail[0] is not None:
                        pending_tail[0]()
                        pending_tail[0] = None
                    if ci >= 2:
                        pa, pb = chunks[ci - 2]
                        emit_lpv(h, hh, p16, pa, pb)
                for j in (62, 63):
                    emit_lpv(h, hh, p16, chunks[j][0], chunks[j][1])
                for a in range(8):
                    g = 2 * p16 + (a // 4)
                    half_last_pv[g] = pv_pe[h * 64 + a * 8 + 7]
                head_last_pe[h] = pv_pe[h * 64 + 63]
                hl = head_last_pe[h]
                S.emit("vector", lambda v: v.tensor_scalar_mul(
                    r_tmp[:, 0:512], P[3][0:1, 0:512], L_SHIFT), waits=[("pe", hl)])
                S.emit("vector", lambda v: v.tensor_scalar_mul(
                    r_tmp[:, 512:1024], P[3][0:1, 512:1024], L_SHIFT))
                S.emit("vector", lambda v: v.reciprocal(r_rstd[:, :], r_tmp[:, :]))
                rl = S.emit("vector", lambda v: v.tensor_scalar_mul(
                    r_brow[:, :], r_rstd[:, :], L_SHIFT), inc="ve")

                def tail(h=h, rl=rl):
                    S.emit("tensor", lambda t: t.matmul(
                        P[3][:, 0:512], ones_rf[:, :], r_brow[0:1, 0:512],
                        start=True, stop=True), waits=[("ve", rl)])
                    rb = S.emit("tensor", lambda t: t.matmul(
                        P[3][:, 512:1024], ones_rf[:, :], r_brow[0:1, 512:1024],
                        start=True, stop=True), inc="pe")
                    wrlb = [("pe", rb)]
                    if h >= 1:
                        wrlb.append(("ve", ctx_ve[h - 1]))
                    else:
                        wrlb.append(("ve", a_end["ve"]))
                    rlb_ac[h] = S.emit("scalar", lambda sc: sc.activation(
                        RLB[:, :], P[3][:, :], AF.Identity), inc="ac", waits=wrlb)
                    S.emit("vector", lambda v: v.tensor_mul(
                        CTX[:, h * 1024: h * 1024 + 512], P[2][:, 0:512], RLB[:, 0:512]),
                        waits=[("ac", rlb_ac[h])])
                    ctx_ve[h] = S.emit("vector", lambda v: v.tensor_mul(
                        CTX[:, h * 1024 + 512: (h + 1) * 1024], P[2][:, 512:1024],
                        RLB[:, 512:1024]), inc="ve")
                pending_tail[0] = tail
            pair_last = head_last_pe[2 * p16 + 1]
        pending_tail[0]()
        pending_tail[0] = None
        ctx_all = ctx_ve[31]
        b_end_pe = head_last_pe[31]
        if debug:
            S.dma("sync", lambda sy: sy.dma_start(out=dbg["d_ctx"], in_=CTX[:, :]),
                  waits=[("ve", ctx_all)])

        cmm = {}
        x2_ve = {}
        x2w = {}
        csq = {}
        cstat = {}
        for of in range(32):
            par = of % 2
            wfree = [("pe", cmm[of - 2])] if of >= 2 else \
                    [("pe", b_end_pe), ("ve", ctx_all), ("ac", rlb_ac[31])]
            cwld = S.dma("gpsimd", lambda g, of=of, par=par: g.dma_start(
                out=WST[par][:, :], in_=out_wt[of, :, :]), waits=wfree)
            wx = [("ve", x2_ve[of - 2])] if of >= 2 else [("ve", ctx_all)]
            cxld = S.dma("sync", lambda sy, of=of, par=par: sy.dma_start(
                out=XLD[par][:, :], in_=x_fm[of * 128:(of + 1) * 128, :]), waits=wx)
            w0 = [("gd", cwld), ("ve", ctx_all)]
            if of >= 2:
                w0.append(("ve", x2_ve[of - 2]))
            for kc in range(32):
                st0, sp0 = (kc == 0), (kc == 31)
                S.emit("tensor", lambda t, par=par, kc=kc, st0=st0, sp0=sp0: t.matmul(
                    P[par][:, 0:512], WST[par][:, kc * 128:(kc + 1) * 128],
                    CTX[:, kc * 1024: kc * 1024 + 512], start=st0, stop=sp0),
                    waits=(w0 if kc == 0 else []))
                cmm[of] = S.emit("tensor", lambda t, par=par, kc=kc, st0=st0, sp0=sp0: t.matmul(
                    P[par][:, 512:1024], WST[par][:, kc * 128:(kc + 1) * 128],
                    CTX[:, kc * 1024 + 512: (kc + 1) * 1024], start=st0, stop=sp0),
                    inc="pe" if sp0 else None)
            wv = [("pe", cmm[of]), ("qd", cxld)]
            if of >= 2:
                wv += [("qd", x2w[of - 2]), ("pe", cstat[of - 2]), ("ac", csq[of - 2])]
            x2_ve[of] = S.emit("vector", lambda v, par=par, of=of: v.scalar_tensor_tensor(
                X2S[par][:, :], P[par][:, :], outb[:, of:of + 1], XLD[par][:, :],
                ALU.add, ALU.add), inc="ve", waits=wv)
            x2w[of] = S.dma("sync", lambda sy, par=par, of=of: sy.dma_start(
                out=x2_sp[of * 128:(of + 1) * 128, :], in_=X2S[par][:, :]),
                waits=[("ve", x2_ve[of])])
            wsq = [("ve", x2_ve[of])]
            if of >= 1:
                wsq.append(("pe", cstat[of - 1]))
            csq[of] = S.emit("scalar", lambda s, par=par: s.activation(
                SQF[:, :], X2S[par][:, :], AF.Square), inc="ac", waits=wsq)
            st0, sp0 = (of == 0), (of == 31)
            S.emit("tensor", lambda t, par=par, st0=st0, sp0=sp0: t.matmul(
                P[2][0:1, 0:512], ones_cf[:, :], X2S[par][:, 0:512], start=st0, stop=sp0),
                waits=[("ve", x2_ve[of]), ("ac", csq[of])])
            S.emit("tensor", lambda t, par=par, st0=st0, sp0=sp0: t.matmul(
                P[2][0:1, 512:1024], ones_cf[:, :], X2S[par][:, 512:1024], start=st0, stop=sp0))
            S.emit("tensor", lambda t, st0=st0, sp0=sp0: t.matmul(
                P[3][0:1, 0:512], ones_cf[:, :], SQF[:, 0:512], start=st0, stop=sp0))
            cstat[of] = S.emit("tensor", lambda t, st0=st0, sp0=sp0: t.matmul(
                P[3][0:1, 512:1024], ones_cf[:, :], SQF[:, 512:1024], start=st0, stop=sp0),
                inc="pe")
        if debug:
            S.dma("sync", lambda sy: sy.dma_start(out=dbg["d_x2"], in_=x2_sp),
                  waits=[("qd", x2w[31]), ("ve", x2_ve[31])])

        bcast2 = emit_ln_rows(cstat[31])

        ln2_ve = {}
        for kc in range(32):
            sl = kc % 2
            w = [("pe", bcast2), ("qd", x2w[kc])]
            if kc >= 2:
                w.append(("ve", ln2_ve[kc - 2]))
            else:
                w.append(("ve", x2_ve[31]))
            xq2 = S.dma("sync", lambda sy, kc=kc, sl=sl: sy.dma_start(
                out=XLD[sl][:, :], in_=x2_sp[kc * 128:(kc + 1) * 128, :]), waits=w)
            S.emit("vector", lambda v, sl=sl: v.tensor_mul(
                SQF[:, :], XLD[sl][:, :], P[2][:, :]),
                waits=[("qd", xq2), ("pe", bcast2), ("ac", csq[31])])
            ln2_ve[kc] = S.emit("vector", lambda v, kc=kc: v.tensor_sub(
                CTX[:, kc * 1024:(kc + 1) * 1024], SQF[:, :], P[3][:, :]), inc="ve")
        ln2_done = ln2_ve[31]
        if debug:
            S.dma("sync", lambda sy: sy.dma_start(out=dbg["d_ln2"], in_=CTX[:, :]),
                  waits=[("ve", ln2_done)])

        aat_gd = S.dma("gpsimd", lambda g: g.dma_start(
            out=AAT[:, :].rearrange("p (e n) -> p e n", e=2),
            in_=a_allt_t.rearrange("e p n -> p e n")),
            waits=[("pe", b_end_pe), ("ve", ln2_done)])
        for e2 in range(2):
            bfr_gd = S.dma("gpsimd", lambda g, e2=e2: g.dma_start(
                out=BFR[e2][:, :], in_=bfr_t[e2, :, :]))
        wv16 = S.emit("vector", lambda v: v.tensor_mul(
            w16_sb[:, :], cf_sb[:, :], sc_sb[:, :]), inc="ve",
            waits=[("qd", pre_qd), ("ve", ln2_done)])
        wer_ac = {}
        for e2 in range(2):
            wm = S.emit("tensor", lambda t, e2=e2: t.matmul(
                P[3][:, e2:e2 + 1], oh_sb[:, e2 * 128:(e2 + 1) * 128], w16_sb[:, :],
                start=True, stop=True), inc="pe", waits=[("ve", wv16)])
            wer_ac[e2] = S.emit("scalar", lambda s, e2=e2: s.activation(
                wer_sb[:, e2:e2 + 1], P[3][:, e2:e2 + 1], AF.Identity), inc="ac",
                waits=[("pe", wm)])
        xa_ac = {}
        for e2 in range(2):
            w0 = [("gd", aat_gd), ("ve", ln2_done)]
            if e2 == 1:
                w0.append(("ac", xa_ac[0]))
            xam = None
            for kc in range(32):
                st0, sp0 = (kc == 0), (kc == 31)
                S.emit("tensor", lambda t, e2=e2, kc=kc, st0=st0, sp0=sp0: t.matmul(
                    P[2][:, 0:512], AAT[:, (e2 * 32 + kc) * 128:(e2 * 32 + kc + 1) * 128],
                    CTX[:, kc * 1024: kc * 1024 + 512], start=st0, stop=sp0),
                    waits=(w0 if kc == 0 else []))
                xam = S.emit("tensor", lambda t, e2=e2, kc=kc, st0=st0, sp0=sp0: t.matmul(
                    P[2][:, 512:1024], AAT[:, (e2 * 32 + kc) * 128:(e2 * 32 + kc + 1) * 128],
                    CTX[:, kc * 1024 + 512: (kc + 1) * 1024], start=st0, stop=sp0),
                    inc="pe" if sp0 else None)
            xa_ac[e2] = S.emit("scalar", lambda s, e2=e2: s.activation(
                XA[e2][:, :], P[2][:, :], AF.Identity, scale=wer_sb[:, e2:e2 + 1]),
                inc="ac", waits=[("pe", xam), ("ac", wer_ac[e2])])

        dmm = {}
        dout_ve = {}
        outw = {}
        for of in range(32):
            par = of % 2
            wfree = [("pe", dmm[of - 2])] if of >= 2 else [("pe", cmm[30 + of])]
            dwld = S.dma("gpsimd", lambda g, of=of, par=par: g.dma_start(
                out=WST[par][:, :], in_=ffn_wtt[of, :, :]), waits=wfree)
            wx = [("qd", x2w[of])]
            if of >= 2:
                wx.append(("ve", dout_ve[of - 2]))
            else:
                wx.append(("ve", ln2_done))
            dxld = S.dma("sync", lambda sy, of=of, par=par: sy.dma_start(
                out=XLD[par][:, :], in_=x2_sp[of * 128:(of + 1) * 128, :]), waits=wx)
            w0 = [("gd", dwld), ("ac", xa_ac[1]), ("ve", ln2_done)]
            if of >= 2:
                w0.append(("ve", dout_ve[of - 2]))
            for kc in range(32):
                st0 = (kc == 0)
                S.emit("tensor", lambda t, par=par, kc=kc, st0=st0: t.matmul(
                    P[par][:, 0:512], WST[par][:, kc * 128:(kc + 1) * 128],
                    CTX[:, kc * 1024: kc * 1024 + 512], start=st0, stop=False),
                    waits=(w0 if kc == 0 else []))
                S.emit("tensor", lambda t, par=par, kc=kc, st0=st0: t.matmul(
                    P[par][:, 512:1024], WST[par][:, kc * 128:(kc + 1) * 128],
                    CTX[:, kc * 1024 + 512: (kc + 1) * 1024], start=st0, stop=False))
            for e2 in range(2):
                sp0 = (e2 == 1)
                S.emit("tensor", lambda t, par=par, e2=e2, of=of, sp0=sp0: t.matmul(
                    P[par][:, 0:512], BFR[e2][:, of * 128:(of + 1) * 128],
                    XA[e2][:, 0:512], start=False, stop=sp0))
                dmm[of] = S.emit("tensor", lambda t, par=par, e2=e2, of=of, sp0=sp0: t.matmul(
                    P[par][:, 512:1024], BFR[e2][:, of * 128:(of + 1) * 128],
                    XA[e2][:, 512:1024], start=False, stop=sp0),
                    inc="pe" if sp0 else None)
            wv = [("pe", dmm[of]), ("qd", dxld)]
            if of >= 2:
                wv.append(("qd", outw[of - 2]))
            dout_ve[of] = S.emit("vector", lambda v, par=par, of=of: v.scalar_tensor_tensor(
                X2S[par][:, :], P[par][:, :], ffnb[:, of:of + 1], XLD[par][:, :],
                ALU.add, ALU.add), inc="ve", waits=wv)
            outw[of] = S.dma("sync", lambda sy, par=par, of=of: sy.dma_start(
                out=out_fm[of * 128:(of + 1) * 128, :], in_=X2S[par][:, :]),
                waits=[("ve", dout_ve[of])])

        S.emit("sync", None, waits=[("qd", S.n["qd"])])

        block.tensor(lambda e: S.materialize(e, "tensor", sems))
        block.vector(lambda e: S.materialize(e, "vector", sems))
        block.scalar(lambda e: S.materialize(e, "scalar", sems))
        block.gpsimd(lambda e: S.materialize(e, "gpsimd", sems))
        block.sync(lambda e: S.materialize(e, "sync", sems))

    return nc



def _prep_inputs(inputs):
    x = np.asarray(inputs["L0_attn_hidden_states"], dtype=np.float32)[0]
    qkv_w = np.asarray(inputs["L0_attn_qkv_w"], dtype=np.float32)
    qkv_b = np.asarray(inputs["L0_attn_qkv_b"], dtype=np.float32)
    out_w = np.asarray(inputs["L0_attn_out_w"], dtype=np.float32)
    out_b = np.asarray(inputs["L0_attn_out_b"], dtype=np.float32)
    ffn_W = np.asarray(inputs["L0_ffn_W"], dtype=np.float32)
    ffn_b = np.asarray(inputs["L0_ffn_bias"], dtype=np.float32)
    A = np.asarray(inputs["L0_ffn_A"], dtype=np.float32)
    Bf = np.asarray(inputs["L0_ffn_Bf"], dtype=np.float32)
    cf = np.ascontiguousarray(np.asarray(inputs["L0_ffn_coeffs"], dtype=np.float32).reshape(16, 1))
    sc = np.ascontiguousarray(np.asarray(inputs["L0_ffn_scales"], dtype=np.float32).reshape(16, 1))

    C = np.ascontiguousarray
    qkv_wt = C(qkv_w.reshape(32, 128, 96, 128).transpose(2, 1, 0, 3).reshape(96, 128, 4096))
    out_wt = C(out_w.reshape(32, 128, 32, 128).transpose(2, 1, 0, 3).reshape(32, 128, 4096))
    ffn_wtt = C(ffn_W.T.reshape(32, 128, 32, 128).transpose(2, 1, 0, 3).reshape(32, 128, 4096))
    a_allt_t = C(A.reshape(256, 4096).T.reshape(32, 128, 2, 128)
                 .transpose(2, 1, 0, 3).reshape(2, 128, 4096))
    bfr_t = C(Bf.transpose(0, 2, 1).reshape(2, 128, 4096))
    qkvb_fm = C(qkv_b.reshape(96, 128).T)
    outb_fm = C(out_b.reshape(32, 128).T)
    ffnb_fm = C(ffn_b.reshape(32, 128).T)

    oh = np.zeros((16, 256), dtype=np.float32)
    for e2 in range(2):
        for p in range(128):
            oh[e2 * 8 + p // 16, e2 * 128 + p] = 1.0
    ident = np.eye(128, dtype=np.float32)
    swap_m = np.zeros((128, 128), dtype=np.float32)
    swap_m[np.arange(128), (np.arange(128) + 64) % 128] = 1.0
    inv = 10000.0 ** (-np.arange(64, dtype=np.float64) / 64.0)

    in_maps = []
    token_idx_all = []
    for c in range(NCORE):
        blocks = [8 * t + c for t in range(NT)]
        tok = np.concatenate([np.arange(b * 128, (b + 1) * 128) for b in blocks])
        token_idx_all.append(tok)
        ang = np.outer(inv, tok.astype(np.float64))
        cs = np.concatenate([np.cos(ang), np.cos(ang)], axis=0).astype(np.float32)
        csw = np.concatenate([np.sin(ang), -np.sin(ang)], axis=0).astype(np.float32)
        maskT = np.zeros((128, 1024), dtype=np.float32)
        for b in range(8):
            if b == c:
                maskT[:, b * 128:(b + 1) * 128] = np.where(
                    np.arange(128)[:, None] > np.arange(128)[None, :],
                    np.float32(MASKV), np.float32(0.0))
            elif b > c:
                maskT[:, b * 128:(b + 1) * 128] = MASKV
        in_maps.append({
            "x_fm": C(x[tok, :].T),
            "qkv_wt": qkv_wt, "qkvb_fm": qkvb_fm,
            "out_wt": out_wt, "outb_fm": outb_fm,
            "ffn_wtt": ffn_wtt, "ffnb_fm": ffnb_fm,
            "a_allt_t": a_allt_t, "bfr_t": bfr_t,
            "cf_in": cf, "sc_in": sc,
            "cs_in": cs, "csw_in": csw,
            "maskT_in": maskT, "id_in": ident, "oh_in": oh, "swap_in": swap_m,
        })
    return in_maps, token_idx_all


def kernel(**inputs):
    in_maps, token_idx_all = _prep_inputs(inputs)
    nc = build_program(debug=False)
    res = run_bass_kernel_spmd(nc, in_maps, core_ids=list(range(NCORE)))
    out = np.empty((SEQ, D), dtype=np.float32)
    for c in range(NCORE):
        out[token_idx_all[c], :] = res.results[c]["out_fm"].T
    return out.reshape(1, SEQ, D)
